# version 24
# baseline (speedup 1.0000x reference)
import numpy as np
import ml_dtypes
import concourse.bass as bass
import concourse.mybir as mybir
from concourse.bass_utils import run_bass_kernel_spmd
from concourse.bass import IndirectOffsetOnAxis

F32 = mybir.dt.float32
BF16 = mybir.dt.bfloat16
AF = mybir.ActivationFunctionType
ALU = mybir.AluOpType
AX = mybir.AxisListType

D = 1024
NCH = 8
NTB_ALL = 128
NTB_OWN = 32
FF = 4096
NE = 34
NQ = 4
QTB = 8
EPS = 1e-5
QK_SCALE = 0.25
CAP = 1536
TPE = CAP // 128

ENGS = ("pe", "act", "dve", "sy", "gp")


class Prog:
    def __init__(self):
        self.ops = {e: [] for e in ENGS}
        self.ctr = {}
        self.sem_names = set()

    def op(self, eng, fn, waits=(), inc=None, inc_n=1):
        w = [x for x in waits if x is not None]
        for (s, v) in w:
            self.sem_names.add(s)
        self.ops[eng].append((fn, w, inc, inc_n))
        if inc is not None:
            self.sem_names.add(inc)
            self.ctr[inc] = self.ctr.get(inc, 0) + inc_n
            return (inc, self.ctr[inc])
        return None

    def emit(self, block, sems):
        def mk(name):
            def run(engine):
                for fn, waits, inc, inc_n in self.ops[name]:
                    for (s, v) in waits:
                        engine.wait_ge(sems[s], v)
                    ins = fn(engine) if fn is not None else None
                    if inc is not None:
                        ins.then_inc(sems[inc], inc_n)
            return run
        block.tensor(mk("pe"))
        block.scalar(mk("act"))
        block.vector(mk("dve"))
        block.sync(mk("sy"))
        block.gpsimd(mk("gp"))


class Stream:

    def __init__(self, prog, name, qeng):
        self.p, self.name, self.qeng = prog, name, qeng
        self.n = 0

    def dma(self, out, in_, waits=()):
        w = list(waits)
        if self.n > 0:
            w.append((self.name, 16 * self.n))
        m = self.p.op(self.qeng, lambda e, o=out, i=in_: e.dma_start(out=o, in_=i),
                      waits=w, inc=self.name, inc_n=16)
        self.n += 1
        return m


def _r3(ap, p=128):
    return ap.rearrange("(c p) f -> p c f", p=p)


def build(dbg=False, upto='E', dsub=9):
    nc = bass.Bass("TRN2", target_bir_lowering=False, detect_race_conditions=False)

    def din(name, shape, dt=F32):
        return nc.dram_tensor(name, shape, dt, kind="ExternalInput").ap()

    x_in = din("x", [NTB_ALL * 128, D])
    wq_in = din("wq", [D, D]); wk_in = din("wk", [D, D]); wv_in = din("wv", [D, D])
    wo_in = din("wo", [D, D]); wd_in = din("wd", [D, D])
    ff1_in = din("ff1", [D, FF]); ff2_in = din("ff2", [FF, D])
    g1_in = din("g1", [128, 8]); b1_in = din("b1", [128, 8])
    g2_in = din("g2", [128, 8]); b2_in = din("b2", [128, 8])
    bff1_in = din("bff1", [128, 32])
    bo_rep_in = din("bo_rep", [128, D])
    bff2_rep_in = din("bff2_rep", [128, D])
    bd_rep_in = din("bd_rep", [128, D])
    esc_in = din("esc_rep", [128, 32])
    gwT_in = din("gwT", [D, 32])
    egu_in = din("egu", [NE, D, D])
    edn_in = din("edn", [NE, 512, D])
    identb_in = din("identb", [128, 128], BF16)
    identf_in = din("identf", [128, 128])
    lt128_in = din("lt128", [128, 128], BF16)
    lt32_in = din("lt32", [32, 32], BF16)
    eoffm1_in = din("eoffm1", [128, 32])
    I32 = mybir.dt.int32

    out_dram = nc.dram_tensor("out", [NTB_OWN * 128, D], F32, kind="ExternalOutput").ap()
    spk = "ExternalOutput" if dbg else "Internal"
    ybo_sp = nc.dram_tensor("ybo_sp", [NTB_OWN * 128, D], F32, kind=spk).ap()
    q_sp = nc.dram_tensor("q_sp", [NTB_OWN, 128, D], BF16, kind=spk).ap()
    xat_sp = nc.dram_tensor("xat_sp", [NTB_OWN * 128, D], F32, kind=spk).ap()
    h2_sp = nc.dram_tensor("h2_sp", [NCH, 128, NTB_OWN * 128], BF16, kind=spk).ap()
    x3_sp = nc.dram_tensor("x3_sp", [NTB_OWN * 128, D], F32, kind=spk).ap()
    o1_sp = nc.dram_tensor("o1_sp", [NTB_OWN * 128, D], F32, kind=spk).ap()
    xm_sp = nc.dram_tensor("xm_sp", [NCH, 128, NTB_OWN * 128], BF16, kind=spk).ap()
    xmt_sp = nc.dram_tensor("xmt_sp", [NTB_OWN * 128, D], BF16, kind=spk).ap()
    xg_dram = nc.dram_tensor("xg_dram", [32 * CAP + 1, D], BF16, kind="Internal").ap()
    egu_bf = nc.dram_tensor("egu_bf", [NE, D, D], BF16, kind="Internal").ap()
    edn_bf = nc.dram_tensor("edn_bf", [NE, 512, D], BF16, kind="Internal").ap()
    crow_dram = nc.dram_tensor("crow_dram", [1, NTB_OWN * 32], F32, kind="Internal").ap()
    brow_dram = nc.dram_tensor("brow_dram", [NTB_OWN, 32], F32, kind="Internal").ap()
    outp_dram = nc.dram_tensor("outp_dram", [32 * CAP + 1, D], BF16, kind=spk if dbg else "Internal").ap()
    if dbg:
        ctx_db = nc.dram_tensor("ctx_db", [128, 2048], F32, kind="ExternalOutput").ap()
        z_db = nc.dram_tensor("z_db", [128, 8], F32, kind="ExternalOutput").ap()
        cmb_db = nc.dram_tensor("cmb_db", [128, NTB_OWN * NE], F32, kind="ExternalOutput").ap()
        posi_db = nc.dram_tensor("posi_db", [128, 6 * NTB_OWN], I32, kind="ExternalOutput").ap()
        wgtj_db = nc.dram_tensor("wgtj_db", [128, 6 * NTB_OWN], F32, kind="ExternalOutput").ap()

    SB_OFS = 16384

    def sb(name, shape, dt, at):
        return nc.alloc_sbuf_tensor_at(name, shape, dt, offset=SB_OFS + at)

    identb = sb("identb", [128, 128], BF16, 0)
    identf = sb("identf", [128, 128], F32, 256)
    onesb = sb("onesb", [128, 8], BF16, 768)
    g1c = sb("g1c", [128, 8], F32, 800)
    b1c = sb("b1c", [128, 8], F32, 832)
    g2c = sb("g2c", [128, 8], F32, 864)
    b2c = sb("b2c", [128, 8], F32, 896)
    bff1c = sb("bff1c", [128, 32], F32, 928)
    gwTs = sb("gwTs", [128, 256], F32, 1056)
    esc = sb("esc", [128, 32], F32, 2080)
    stat = sb("stat", [128, 32], F32, 2208)
    qs8 = sb("qs8", [128, 8], F32, 2336)
    zr8 = sb("zr8", [128, 8], F32, 2368)
    cst4 = sb("cst4", [128, 8], F32, 2400)
    nc.const_aps.register(F32, 0.0, cst4[:, 0:1])
    nc.const_aps.register(F32, 1.0, cst4[:, 1:2])
    nc.const_aps.register(F32, 0.25, cst4[:, 2:3])
    nc.const_aps.register(F32, 1.0 / D, cst4[:, 3:4])
    nc.const_aps.register(F32, EPS, cst4[:, 4:5])
    nc.const_aps.register(F32, -0.5, cst4[:, 5:6])
    nc.const_aps.register(F32, EPS * EPS, cst4[:, 6:7])
    nc.const_aps.register(F32, 1.0 + EPS, cst4[:, 7:8])
    xt = [sb(f"xt{p}", [128, D], F32, 2432 + 4096 * p) for p in range(2)]
    yt = [sb(f"yt{p}", [128, D], F32, 10624 + 4096 * p) for p in range(2)]
    yb = [sb(f"yb{p}", [128, D], F32, 18816 + 4096 * p) for p in range(2)]
    qe = sb("qe", [128, D], F32, 27008)
    bfa = sb("bfa", [128, D], BF16, 31104)
    bfb = sb("bfb", [128, D], BF16, 33152)
    bfc = [sb(f"bfc{p}", [128, D], BF16, 35200 + 2048 * p) for p in range(2)]
    bfd = [sb(f"bfd{p}", [128, D], BF16, 39296 + 2048 * p) for p in range(2)]
    qsf = sb("qsf", [128, D], BF16, 43392)
    qfm = [sb(f"qfm{p}", [128, D], BF16, 45440 + 2048 * p) for p in range(2)]
    xm32 = sb("xm32", [128, D], F32, 49536)
    MID = 53632
    ctx_sb = sb("ctx_sb", [128, 2048], F32, MID)
    zsb = sb("zsb", [128, 8], F32, MID + 8192)
    ctxn = sb("ctxn", [128, 2048], BF16, MID + 8224)
    a_fm = sb("a_fm", [128, 8192], BF16, MID)
    combine = sb("combine", [128, NTB_OWN * NE], F32, MID)
    rsc1 = sb("rsc1", [128, NTB_OWN * NE], F32, MID + 4352)
    rsc2 = sb("rsc2", [128, NTB_OWN * NE], F32, MID + 8704)
    h2sl = [sb(f"h2sl{p}", [128, 2048], BF16, 10624 + 4096 * p) for p in range(2)]
    BASE = 70048
    SLOT = 65536
    assert 16384 + BASE + 2 * SLOT <= 229344
    wq = sb("wq", [128, NCH * D], BF16, BASE)
    wk = sb("wk", [128, NCH * D], BF16, BASE + 16384)
    wv = sb("wv", [128, NCH * D], BF16, BASE + 32768)
    ff1 = sb("ff1", [128, NCH * FF], BF16, BASE)
    wd = sb("wd", [128, NCH * D], BF16, BASE)
    bd_rep = sb("bd_rep", [128, D], F32, BASE + 16384)
    acc = sb("acc", [128, QTB * D], F32, BASE)
    xmq = [sb(f"xmq{p}", [128, NCH * QTB * 128], BF16, BASE + 32768 + 16384 * p) for p in range(2)]
    wo = sb("wo", [128, NCH * D], BF16, BASE + SLOT)
    bo_rep = sb("bo_rep", [128, D], F32, BASE + SLOT + 16384)
    bff2_rep = sb("bff2_rep", [128, D], F32, BASE + SLOT + 20480)
    ff2 = sb("ff2", [128, 32 * D], BF16, BASE + SLOT)
    egu = [sb(f"egu{p}", [128, NCH * D], BF16, BASE + SLOT + 16384 * p) for p in range(2)]
    edn = [sb(f"edn{p}", [128, 4 * D], BF16, BASE + SLOT + 32768 + 8192 * p) for p in range(2)]
    TOP = BASE + 2 * SLOT
    qeB = sb("qeB", [128, D], F32, TOP)
    qeD = sb("qeD", [128, D], F32, TOP + 4096)
    obf = [sb(f"obf{i}", [128, D], BF16, TOP + 2048 * i) for i in range(4)]
    qeD2 = sb("qeD2", [128, D], F32, BASE + SLOT + 49152)
    ybb = [sb(f"ybb{i}", [128, D], BF16, 18816 + 2048 * i) for i in range(4)]
    qfm4e = [sb(f"qfm{i}e", [128, D], BF16, 49536 + 2048 * i) for i in range(2)]
    qeC = sb("qeC", [128, D], F32, 49536)
    qex = [sb(f"qex{p}", [128, D], F32, MID + 4096 * p) for p in range(2)]
    A1 = BASE + SLOT + 24576
    xt2 = [sb(f"xt{p+2}", [128, D], F32, A1 + 4096 * p) for p in range(2)]
    yt2 = [sb(f"yt{p+2}", [128, D], F32, A1 + 8192 + 4096 * p) for p in range(2)]
    yb2 = [sb(f"yb{p+2}", [128, D], F32, A1 + 16384 + 4096 * p) for p in range(2)]
    qeE = sb("qeE", [128, D], F32, A1 + 24576)
    qeF = sb("qeF", [128, D], F32, A1 + 28672)
    qeG = sb("qeG", [128, D], F32, A1 + 32768)
    qeH = sb("qeH", [128, D], F32, A1 + 36864)
    stat2 = sb("stat2", [128, 32], F32, MID + 12320)
    posi = sb("posi", [128, 6 * 32], I32, MID + 13056)
    wgtj = sb("wgtj", [128, 6 * 32], F32, MID + 13824)
    lt32 = sb("lt32", [32, 32], BF16, MID + 14592)
    eoffm1 = sb("eoffm1", [128, 32], F32, MID + 14656)
    lt128 = sb("lt128", [128, 128], BF16, MID + 14784)
    onesf = sb("onesf", [1, 128], F32, MID + 15040)
    mks = sb("mks", [128, 6 * NTB_OWN * 32], BF16, BASE + SLOT + 49152)
    ctb = sb("ctb", [32, 32], BF16, 10624 + 4096)
    bsb = sb("bsb", [32, 32], F32, 18816)
    gbt = [sb(f"gbt{i}", [128, D], BF16, BASE + SLOT + 2048 * i) for i in range(16)]
    bfa1 = sb("bfa1", [128, D], BF16, BASE + 49152)
    bfb1 = sb("bfb1", [128, D], BF16, BASE + 51200)
    bfa2_ = sb("bfa2_", [128, D], BF16, BASE + 53248)
    bfa3_ = sb("bfa3_", [128, D], BF16, BASE + 55296)
    bfb2_ = sb("bfb2_", [128, D], BF16, BASE + 57344)
    bfb3_ = sb("bfb3_", [128, D], BF16, BASE + 59392)
    assert 16384 + TOP + 8192 <= 229344
    assert A1 + 40960 <= BASE + 2 * SLOT

    PS = nc.alloc_psum_tensor("PS", [128, 4096], F32)

    def bank(b, n=1):
        return PS[:, b * 512:(b + n) * 512]

    PSb = PS[:].bitcast(BF16)

    def bankb(b):
        return PSb[:, b * 1024:(b + 1) * 1024]

    p = Prog()
    SPE, SACT, SDVE = "spe", "sact", "sdve"

    st_x = [Stream(p, f"sx{i}", "sy") for i in range(4)]
    st_y = [Stream(p, f"sy{i}", "sy") for i in range(2)]
    st_q = [Stream(p, f"sq{i}", "sy") for i in range(2)]
    st_a = [Stream(p, f"sa{i}", "gp") for i in range(2)]
    st_h = [Stream(p, f"sh{i}", "gp") for i in range(2)]
    st_3 = [Stream(p, f"s3{i}", "sy") for i in range(2)]
    st_o = [Stream(p, f"so{i}", "sy") for i in range(2)]
    st_m = [Stream(p, f"sm{i}", "gp") for i in range(2)]
    st_w = [Stream(p, f"sw{i}", "sy") for i in range(2)]
    st_t = [Stream(p, f"st{i}", "gp") for i in range(2)]
    st_sc = Stream(p, "ssc", "gp")
    st_s4 = [Stream(p, f"sc{i}s", "gp") for i in range(4)]
    st_gt = [Stream(p, f"sg{i}x", "gp") for i in range(8)]
    st_c = Stream(p, "scst", "sy")
    st_g = Stream(p, "sgw", "gp")
    st_e = [Stream(p, f"se{i}", "gp") for i in range(4)]
    st_n = [Stream(p, f"sn{i}", "gp") for i in range(2)]
    st_pc = [Stream(p, f"pc{i}", "gp") for i in range(4)]

    st_c.dma(identb[:], identb_in)
    st_c.dma(g1c[:], g1_in)
    st_c.dma(b1c[:], b1_in)
    st_c.dma(g2c[:], g2_in)
    st_c.dma(b2c[:], b2_in)
    st_c.dma(bff1c[:], bff1_in)
    m_bo = st_c.dma(bo_rep[:], bo_rep_in)
    st_c.dma(identf[:], identf_in)
    C_IDENTB = ("scst", 16)
    C_LN1 = ("scst", 16 * 3)
    C_LN2 = ("scst", 16 * 5)
    C_BFF1 = ("scst", 16 * 6)
    C_IDENTF = ("scst", 16 * 8)
    C_ESC = ("scst", 16 * 9)

    p.op("dve", lambda e: e.memset(onesb[:], 1.0), inc=SDVE)
    p.op("dve", lambda e: e.memset(cst4[:, 0:1], 0.0), inc=SDVE)
    p.op("dve", lambda e: e.memset(cst4[:, 1:2], 1.0), inc=SDVE)
    p.op("dve", lambda e: e.memset(cst4[:, 2:3], 0.25), inc=SDVE)
    p.op("dve", lambda e: e.memset(cst4[:, 3:4], 1.0 / D), inc=SDVE)
    p.op("dve", lambda e: e.memset(cst4[:, 4:5], EPS), inc=SDVE)
    p.op("dve", lambda e: e.memset(cst4[:, 5:6], -0.5), inc=SDVE)
    p.op("dve", lambda e: e.memset(cst4[:, 6:7], EPS * EPS), inc=SDVE)
    p.op("dve", lambda e: e.memset(cst4[:, 7:8], 1.0 + EPS), inc=SDVE)

    st_g2 = [Stream(p, f"g2{i}", "gp") for i in range(3)]
    m_wq = st_g.dma(wq[:].rearrange("p (c f) -> p c f", c=NCH), _r3(wq_in))
    m_wk = st_g2[0].dma(wk[:].rearrange("p (c f) -> p c f", c=NCH), _r3(wk_in))
    m_wv = st_g2[1].dma(wv[:].rearrange("p (c f) -> p c f", c=NCH), _r3(wv_in))
    m_wo = st_g2[2].dma(wo[:].rearrange("p (c f) -> p c f", c=NCH), _r3(wo_in))

    ln_prev = {}
    qe_rd = {}

    def ln_stats(src, base, m_ready, scratch, dve_extra=(), act_extra=(), stile=None):
        st_ = stat if stile is None else stile
        key = (st_.name, base)
        s1 = st_[:, base:base + 1]; m2 = st_[:, base + 1:base + 2]
        v1 = st_[:, base + 2:base + 3]; sd = st_[:, base + 3:base + 4]
        rstd = st_[:, base + 4:base + 5]; bv = st_[:, base + 5:base + 6]
        lnv = st_[:, base + 6:base + 7]
        ma1 = p.op("dve", lambda e, s=src, q=scratch:
                   e.tensor_tensor(q[:], s, s, ALU.mult),
                   waits=list(act_extra) + [m_ready, qe_rd.get(scratch.name)], inc=SDVE)
        md1 = p.op("dve", lambda e, s=src, o=s1: e.reduce_sum(o, s, axis=AX.X),
                   waits=[m_ready, ln_prev.get(key)] + list(dve_extra), inc=SDVE)
        mdr = p.op("dve", lambda e, o=v1, q=scratch: e.reduce_sum(o, q[:], axis=AX.X),
                   waits=[ma1], inc=SDVE)
        qe_rd[scratch.name] = mdr
        md2 = p.op("dve", lambda e, i=s1, o=m2:
                   e.scalar_tensor_tensor(o, i, 1.0 / (D * D), i, ALU.mult, ALU.mult),
                   waits=[md1], inc=SDVE)
        md3 = p.op("dve", lambda e, i=v1, o=sd, m=m2:
                   e.scalar_tensor_tensor(o, i, 1.0 / D, m, ALU.mult, ALU.subtract),
                   waits=[mdr, md2], inc=SDVE)
        ma2 = p.op("act", lambda e, i=sd, o=lnv:
                   e.activation(o, i, AF.Ln, bias=EPS, scale=1.0), waits=[md3], inc=SACT)
        ma3 = p.op("act", lambda e, i=lnv, o=rstd:
                   e.activation(o, i, AF.Exp, bias=0.0, scale=-0.5), inc=SACT)
        md4 = p.op("dve", lambda e, i=s1, r=rstd, o=bv:
                   e.scalar_tensor_tensor(o, i, -1.0 / D, r, ALU.mult, ALU.mult),
                   waits=[ma3], inc=SDVE)
        ln_prev[key] = md1
        return rstd, bv, md1, md4, s1, sd, md3

    m_y_done = [None] * 4
    m_yt_dve = [None] * 4
    m_hn = {}
    m_tr = {}
    m_hev = {}
    m_k = {}
    m_v = {}
    m_q = {}
    m_ek = {}
    m_vv = {}
    m_qe = {}
    m_ctxm = {}
    m_mq3 = {}
    m_kq_free = None
    ost4 = [qe, qeB, qeE, qeF]
    ist4 = [qeC, qeD, qeG, qeH]
    bfa2 = [bfa, bfa1, bfa2_, bfa3_]
    bfb2 = [bfb, bfb1, bfb2_, bfb3_]
    qsf2 = [qsf, qfm[0]]
    xt4 = xt + xt2
    yt4 = yt + yt2
    yb4 = yb + yb2
    stile4 = [stat, stat, stat2, stat2]
    nq_own = [0]

    def stats_block(i):
        par4 = i % 4
        par2 = i % 2
        own = i < NTB_OWN
        st_x[par4].dma(xt4[par4][:], x_in[i * 128:(i + 1) * 128, :],
                       waits=[m_y_done[par4]])
        rstd1, bv1, md1o, md4o, s1o, vo, md3o = ln_stats(xt4[par4][:], (par4 % 2) * 8,
                                       (f"sx{par4}", 16 * st_x[par4].n),
                                       ost4[par4], stile=stile4[par4])
        if own:
            my = p.op("act", lambda e, s=xt4[par4][:], o=yt4[par4][:], r=rstd1, b=bv1:
                      e.activation(o, s, AF.Identity, bias=b, scale=r),
                      waits=[md4o, m_yt_dve[par4]], inc=SACT)
            w_b = [my, m_bo]
            if i >= 4:
                w_b.append((f"sy{par2}", 16 * st_y[par2].n))
            mb = p.op("dve", lambda e, y=yt4[par4][:], o=yb4[par4][:]:
                      e.tensor_tensor(o, y, bo_rep[:], ALU.add), waits=w_b, inc=SDVE)
            st_y[par2].dma(ybo_sp[i * 128:(i + 1) * 128, :], yb4[par4][:], waits=[mb])
            m_yt_dve[par4] = mb
        sB = 16 + (par4 % 2) * 8
        r2t = stile4[par4][:, sB:sB + 1]
        bvh = stile4[par4][:, sB + 1:sB + 2]
        mh2 = p.op("act", lambda e, i_=vo, o=r2t:
                   e.activation(o, i_, AF.Ln, bias=EPS * EPS, scale=1.0 + EPS),
                   waits=[md3o], inc=SACT)
        mh3 = p.op("act", lambda e, i_=r2t, o=r2t:
                   e.activation(o, i_, AF.Exp, bias=0.0, scale=-0.5), inc=SACT)
        mh4 = p.op("dve", lambda e, i_=s1o, r=r2t, o=bvh:
                   e.scalar_tensor_tensor(o, i_, -1.0 / D, r, ALU.mult, ALU.mult),
                   waits=[mh3, m_hn.get(i - 4)], inc=SDVE)
        m_hn[i] = p.op("act", lambda e, s=xt4[par4][:], r=r2t, b=bvh, o=bfa2[i % 4][:]:
                       e.activation(o, s, AF.Identity, bias=b, scale=r),
                       waits=[mh4, m_tr.get(i - 4)], inc=SACT)
        m_y_done[par4] = m_hn[i]

    stats_block(0)
    stats_block(1)
    stats_block(2)
    for c in range(NTB_ALL + 2):
        if c < NTB_ALL:
            def tr_h(e, pp=c % 4):
                for ch in range(NCH):
                    mm = e.transpose(bankb(7)[:, ch * 128:(ch + 1) * 128],
                                     bfa2[pp][:, ch * 128:(ch + 1) * 128], identb[:])
                return mm
            m_tr[c] = p.op("pe", tr_h, waits=[m_hn[c], m_hev.get(c - 1), C_IDENTB], inc=SPE)
        if 0 <= c - 1 < NTB_ALL:
            j = c - 1

            def proj(e, wgt, pp):
                for ch in range(NCH):
                    lhs = bfb2[pp][:, ch * 128:(ch + 1) * 128]
                    e.matmul(bank(5), lhs, wgt[:, ch * D:ch * D + 512],
                             start=(ch == 0), stop=(ch == NCH - 1))
                    mm = e.matmul(bank(6), lhs, wgt[:, ch * D + 512:(ch + 1) * D],
                                  start=(ch == 0), stop=(ch == NCH - 1))
                return mm
            m_k[j] = p.op("pe", lambda e, j=j: proj(e, wk, j % 4),
                          waits=[m_hev[j], m_kq_free, m_wk], inc=SPE)
            m_ek[j] = p.op("act", lambda e, o=bfc[j % 2][:]:
                           e.activation(o, bank(5, 2), AF.Exp, scale=QK_SCALE),
                           waits=[m_k[j], m_ctxm.get(j - 2)], inc=SACT)
        if 0 <= c - 2 < NTB_ALL:
            j = c - 2

            def ctxmm(e, first=(j == 0), last=(j == NTB_ALL - 1), pp=j % 2):
                for k in range(8):
                    h = k // 2
                    e.matmul(PS[:, k * 256:(k + 1) * 256], bfc[pp][:, k * 128:(k + 1) * 128],
                             bfd[pp][:, h * 256:(h + 1) * 256],
                             start=(first and k % 2 == 0), stop=(last and k % 2 == 1),
                             skip_group_check=True)
                for k in range(8):
                    mm = e.matmul(PS[:, 2048 + k:2049 + k], bfc[pp][:, k * 128:(k + 1) * 128],
                                  onesb[:, 0:1], start=(first and k == 0),
                                  stop=(last and k == 7), skip_group_check=True)
                return mm
            m_ctxm[j] = p.op("pe", ctxmm, waits=[m_vv[j]], inc=SPE)
        if c + 3 < NTB_ALL:
            stats_block(c + 3)
        if 0 <= c - 1 < NTB_ALL:
            j = c - 1
            m_v[j] = p.op("pe", lambda e, j=j: proj(e, wv, j % 4),
                          waits=[m_ek[j], m_wv], inc=SPE)
            m_vv[j] = p.op("act", lambda e, o=bfd[j % 2][:]:
                           e.activation(o, bank(5, 2), AF.Identity), waits=[m_v[j]], inc=SACT)
            m_kq_free = m_vv[j]
            if c < NTB_ALL:
                pass
        if c < NTB_ALL:
            def hev(e, pp=c % 4):
                for ch in range(NCH):
                    ins = e.scalar_tensor_tensor(
                        bfb2[pp][:, ch * 128:(ch + 1) * 128],
                        bankb(7)[:, ch * 128:(ch + 1) * 128],
                        g1c[:, ch:ch + 1], b1c[:, ch:ch + 1].broadcast_to([128, 128]),
                        ALU.mult, ALU.add)
                return ins
            m_hev[c] = p.op("dve", hev,
                            waits=[m_tr[c], C_LN1, m_q.get(c - 4) or m_v.get(c - 4)],
                            inc=SDVE)
        if 0 <= c - 1 < min(NTB_OWN, NTB_ALL):
            j = c - 1
            qx = qex[j % 2]
            m_q[j] = p.op("pe", lambda e, j=j: proj(e, wq, j % 4),
                          waits=[m_vv[j], m_wq], inc=SPE)
            m_qe[j] = p.op("act", lambda e, o=qx[:]:
                           e.activation(o, bank(5, 2), AF.Exp, scale=QK_SCALE),
                           waits=[m_q[j], qe_rd.get(qx.name)], inc=SACT)
            m_kq_free = m_qe[j]
            mq1 = p.op("dve", lambda e, qx=qx: e.reduce_sum(
                qs8[:, (j % 2) * 4:(j % 2) * 4 + 4],
                qx[:].rearrange("p (h d) -> p h d", h=4), axis=AX.X),
                waits=[m_qe[j]], inc=SDVE)
            mq2 = p.op("dve", lambda e: e.reciprocal(zr8[:, (j % 2) * 4:(j % 2) * 4 + 4],
                                                     qs8[:, (j % 2) * 4:(j % 2) * 4 + 4]),
                       waits=[mq1], inc=SDVE)
            w_q3 = [mq2]
            if j >= 2:
                w_q3.append((f"sq{j % 2}", 16 * st_q[j % 2].n))
            mq3 = p.op("dve", lambda e, qx=qx, o=qsf2[j % 2]: e.tensor_tensor(
                o[:].rearrange("p (h d) -> p h d", h=4),
                qx[:].rearrange("p (h d) -> p h d", h=4),
                zr8[:, (j % 2) * 4:(j % 2) * 4 + 4].unsqueeze(2).broadcast_to([128, 4, 256]),
                ALU.mult), waits=w_q3, inc=SDVE)
            m_mq3[j] = mq3
            qe_rd[qx.name] = mq1
            st_q[j % 2].dma(q_sp[j], qsf2[j % 2][:], waits=[mq3])

    st_c.dma(esc[:], esc_in)
    m_gw = st_c.dma(gwTs[:].rearrange("p (c e) -> p c e", c=8), _r3(gwT_in))
    m_cst = st_c.dma(bff2_rep[:], bff2_rep_in)
    m_pcg = {}
    m_pcd = {}
    order = [32, 33] + list(range(32))
    for i, e_ in enumerate(order):
        w_pc = [m_wo] if i < 2 else [m_wo, m_hn[min(4 * (i - 2) + 8, 124)]]
        m_pcg[e_] = st_pc[i % 2].dma(egu_bf[e_:e_ + 1], egu_in[e_:e_ + 1], waits=w_pc)
        m_pcd[e_] = st_pc[2 + i % 2].dma(edn_bf[e_:e_ + 1], edn_in[e_:e_ + 1], waits=w_pc)

    fA_pe = m_ctxm[NTB_ALL - 1]
    fA_act = (SACT, p.ctr[SACT])
    fA_dve = p.op("dve", lambda e: e.engine_nop(), inc=SDVE)

    if upto >= 'B':
        m_ff1 = st_g.dma(ff1[:].rearrange("p (c f) -> p c f", c=NCH), _r3(ff1_in), waits=[fA_pe])
    def finish():
        waits = []
        for st in (st_x + st_y + st_q + st_a + st_h + st_3 + st_o + st_m + st_w +
                   [st_c, st_g] + st_e + st_n):
            if st.n:
                waits.append((st.name, 16 * st.n))
        waits += [(SPE, p.ctr.get(SPE, 0)), (SACT, p.ctr.get(SACT, 0)), (SDVE, p.ctr.get(SDVE, 0))]
        p.op("sy", None, waits=waits)
        import contextlib
        stack = contextlib.ExitStack()
        sems = {name: stack.enter_context(nc.semaphore(name)) for name in sorted(p.sem_names)}
        with nc.Block() as block:
            p.emit(block, sems)
        return nc

    if upto == 'A':
        return finish()


    me1 = p.op("dve", lambda e: e.tensor_copy(ctx_sb[:], PS[:, 0:2048]), waits=[fA_pe], inc=SDVE)
    me2 = p.op("dve", lambda e: e.tensor_copy(zsb[:], PS[:, 2048:2056]), inc=SDVE)
    me3 = p.op("dve", lambda e: e.reciprocal(zr8[:], zsb[:]), waits=[me2], inc=SDVE)
    me4 = p.op("dve", lambda e, _w=None: e.tensor_tensor(
        ctxn[:].rearrange("p (k e) -> p k e", k=8),
        ctx_sb[:].rearrange("p (k e) -> p k e", k=8),
        zr8[:].unsqueeze(2).broadcast_to([128, 8, 256]), ALU.mult),
        waits=[me3, me1], inc=SDVE)
    if dbg:
        st_c.dma(ctx_db, ctx_sb[:], waits=[me4])
        st_c.dma(z_db, zsb[:], waits=[me4])

    m_attn_ev = None
    m_attn_pe = [None, None]
    m_wo_dve = None
    m_x2n = [None, None]
    m_ptr_free_b = [fA_dve]
    m_tr2 = {}
    m_yb_rd = [None, None]
    m_xat_dve = [None, None]
    m_qtr = [None, None]
    m_qev = [None, None]
    bfaB = [bfa, bfb]
    st_yr = [Stream(p, f"yr{i}", "gp") for i in range(2)]
    m_ry = {}
    for j0 in range(2):
        m_ry[j0] = st_yr[j0].dma(yb[j0][:], ybo_sp[j0 * 128:(j0 + 1) * 128, :],
                                 waits=[(f"sy{i}", 16 * st_y[i].n) for i in range(2)])

    def b_tail(jt):
        pt = jt % 2

        def tr2(e, pt=pt):
            for c in range(NCH):
                mm = e.transpose(bankb(7)[:, c * 128:(c + 1) * 128],
                                 bfaB[pt][:, c * 128:(c + 1) * 128], identb[:])
            return mm
        mt2 = p.op("pe", tr2, waits=[m_x2n[pt], m_ptr_free_b[0]], inc=SPE)
        m_tr2[jt] = mt2

        def h2ev(e, pp=pt):
            for c in range(NCH):
                ins = e.scalar_tensor_tensor(
                    bfc[pp][:, c * 128:(c + 1) * 128], bankb(7)[:, c * 128:(c + 1) * 128],
                    g2c[:, c:c + 1], b2c[:, c:c + 1].broadcast_to([128, 128]),
                    ALU.mult, ALU.add)
            return ins
        w_h2 = [mt2, C_LN2]
        if jt >= 2:
            w_h2.append((f"sh{pt}", 16 * st_h[pt].n))
        mh2 = p.op("dve", h2ev, waits=w_h2, inc=SDVE)
        m_ptr_free_b[0] = mh2
        st_h[pt].dma(h2_sp.rearrange("c d t -> d c t")[:, :, jt * 128:(jt + 1) * 128],
                     bfc[pt][:].rearrange("p (c t) -> p c t", c=NCH), waits=[mh2])

    for j in range(NTB_OWN):
        par = j % 2
        mrq = st_q[par].dma(qfm[par][:], q_sp[j], waits=[m_qtr[par]])
        mry = m_ry[j]

        def tr_qb(e, pp=par):
            for c in range(NCH):
                mm = e.transpose(bankb(4)[:, c * 128:(c + 1) * 128],
                                 qfm[pp][:, c * 128:(c + 1) * 128], identb[:])
            return mm
        w_tq = [mrq, m_qev[par]]
        if j < 2:
            w_tq += [fA_act, fA_dve, me2]
        mtq = p.op("pe", tr_qb, waits=w_tq, inc=SPE)
        m_qtr[par] = mtq
        if j >= 1:
            b_tail(j - 1)
        mqv = p.op("act", lambda e, pp=par: e.activation(bfd[pp][:], bankb(4), AF.Identity),
                   waits=[mtq, m_attn_pe[par]], inc=SACT)
        m_qev[par] = mqv

        def attnmm(e, pp=par):
            for c in range(8):
                h, j2 = c // 2, c % 2
                for j1 in range(2):
                    k = 2 * h + j1
                    mm = e.matmul(PS[:, c * 128:(c + 1) * 128],
                                  ctxn[:, (k * 256 + j2 * 128):(k * 256 + j2 * 128 + 128)],
                                  bfd[pp][:, k * 128:(k + 1) * 128],
                                  start=(j1 == 0 and c % 4 == 0), stop=(j1 == 1),
                                  skip_group_check=True)
            return mm
        ma_ = p.op("pe", attnmm, waits=[me4, mqv, m_attn_ev], inc=SPE)
        m_attn_pe[par] = ma_
        mae = p.op("act", lambda e: e.activation(qsf[:], PS[:, 0:1024], AF.Identity),
                   waits=[ma_], inc=SACT)
        m_attn_ev = mae

        def womm(e):
            for c in range(NCH):
                lhs = qsf[:, c * 128:(c + 1) * 128]
                e.matmul(bank(2), lhs, wo[:, c * D:c * D + 512],
                         start=(c == 0), stop=(c == NCH - 1))
                mm = e.matmul(bank(3), lhs, wo[:, c * D + 512:(c + 1) * D],
                              start=(c == 0), stop=(c == NCH - 1))
            return mm
        mw_ = p.op("pe", womm, waits=[mae, m_wo, m_wo_dve], inc=SPE)
        w_x = [mw_, mry, m_xat_dve[par]]
        mxa = p.op("dve", lambda e, o=yt[par][:], b=yb[par][:]:
                   e.tensor_tensor(o, b, bank(2, 2), ALU.add), waits=w_x, inc=SDVE)
        m_wo_dve = mxa
        if j + 2 < NTB_OWN:
            m_ry[j + 2] = st_yr[par].dma(yb[par][:], ybo_sp[(j + 2) * 128:(j + 3) * 128, :],
                                         waits=[mxa])
        w_x2 = [mxa, m_cst]
        if j >= 2:
            w_x2.append((f"sa{par}", 16 * st_a[par].n))
        mxb = p.op("dve", lambda e, o=xt[par][:], i=yt[par][:]:
                   e.tensor_tensor(o, i, bff2_rep[:], ALU.add), waits=w_x2, inc=SDVE)
        st_a[par].dma(xat_sp[j * 128:(j + 1) * 128, :], xt[par][:], waits=[mxb])
        rstd2, bv2, md1b, md4b, _sB, _vB, _mB = ln_stats(yt[par][:], par * 8, mxa,
                                                         [qe, qeB][par])
        m_xat_dve[par] = md1b
        mx2 = p.op("act", lambda e, s=yt[par][:], r=rstd2, b=bv2, o=bfaB[par][:]:
                   e.activation(o, s, AF.Identity, bias=b, scale=r),
                   waits=[md4b, m_tr2.get(j - 2, fA_pe)], inc=SACT)
        m_x2n[par] = mx2
    b_tail(NTB_OWN - 1)

    fB_pe = (SPE, p.ctr[SPE])
    fB_act = (SACT, p.ctr[SACT])
    fB_dve = (SDVE, p.ctr[SDVE])
    nh = [st_h[0].n, st_h[1].n]
    if upto == 'B':
        return finish()
    m_ff2 = st_g.dma(ff2[:].rearrange("p (c f) -> p c f", c=32), _r3(ff2_in), waits=[fB_pe])

    m_gel = [None, None]
    m_x3dve = [None, None]
    m_ff2_dve = None
    m_pe_after_prev_slice = None
    m_f1_last = [None, None]
    for s in range(NTB_OWN // 2):
        spar = s % 2
        w_rh = [(f"sh{1 - spar}", 16 * nh[1 - spar])]
        if s < 2:
            w_rh += [fB_dve, fB_act]
        else:
            w_rh.append(m_f1_last[spar])
        mrh = st_h[spar].dma(h2sl[spar][:].rearrange("p (c t) -> p c t", c=NCH),
                             h2_sp[:, :, s * 256:(s + 1) * 256].rearrange("c d t -> d c t"),
                             waits=w_rh)
        for fb in range(32):
            bpar = fb % 2

            def f1mm(e, fb=fb, bpar=bpar, spar=spar):
                for c in range(NCH):
                    mm = e.matmul(PS[:, bpar * 512:bpar * 512 + 256],
                                  ff1[:, c * FF + fb * 128:c * FF + fb * 128 + 128],
                                  h2sl[spar][:, c * 256:(c + 1) * 256],
                                  start=(c == 0), stop=(c == NCH - 1))
                return mm
            w_f1 = [mrh, m_ff1, m_gel[bpar]]
            if s == 0 and fb < 2:
                w_f1 += [fB_act, fB_dve]
            mf1 = p.op("pe", f1mm, waits=w_f1, inc=SPE)
            if fb == 31:
                m_f1_last[spar] = mf1
            m_gel[bpar] = p.op("act", lambda e, fb=fb, bpar=bpar:
                               e.activation(a_fm[:, fb * 256:(fb + 1) * 256],
                                            PS[:, bpar * 512:bpar * 512 + 256],
                                            AF.Gelu_apprx_tanh, bias=bff1c[:, fb:fb + 1]),
                               waits=[mf1, C_BFF1], inc=SACT)
        m_gelu_all = m_gel[1]
        for tt_ in range(2):
            j = s * 2 + tt_
            par = j % 2
            mra = st_a[par].dma(xt[par][:], xat_sp[j * 128:(j + 1) * 128, :],
                                waits=[m_x3dve[par]])

            def f2mm(e, tt=tt_):
                for fc in range(32):
                    lhs = a_fm[:, fc * 256 + tt * 128:fc * 256 + tt * 128 + 128]
                    e.matmul(bank(2), lhs, ff2[:, fc * D:fc * D + 512],
                             start=(fc == 0), stop=(fc == 31))
                    mm = e.matmul(bank(3), lhs, ff2[:, fc * D + 512:(fc + 1) * D],
                                  start=(fc == 0), stop=(fc == 31))
                return mm
            w_f2 = [m_gelu_all, m_ff2, m_ff2_dve]
            if s == 0 and tt_ == 0:
                w_f2 += [fB_dve]
            mf2 = p.op("pe", f2mm, waits=w_f2, inc=SPE)
            m_pe_after_prev_slice = mf2
            mx3 = p.op("dve", lambda e, o=yb[par][:], i=xt[par][:]:
                       e.tensor_tensor(o, i, bank(2, 2), ALU.add), waits=[mf2, mra], inc=SDVE)
            m_ff2_dve = mx3
            m_x3dve[par] = mx3
            st_3[par].dma(x3_sp[j * 128:(j + 1) * 128, :], yb[par][:], waits=[mx3])

    fC_pe = (SPE, p.ctr[SPE])
    fC_act = (SACT, p.ctr[SACT])
    fC_dve = (SDVE, p.ctr[SDVE])
    if upto == 'C':
        return finish()
    m_xmsp = {}
    m_wd = st_g.dma(wd[:].rearrange("p (c f) -> p c f", c=NCH), _r3(wd_in), waits=[fC_pe])
    m_bd = st_c.dma(bd_rep[:], bd_rep_in, waits=[fC_pe])

    m_x3T_ev = None
    m_wd_dve = None
    m_xmT_ev = [None]
    m_xmT_act = [None]
    m_o1_dve = [None, None]
    m_trm_prev = None
    m_xm_act = [None, None]
    m_ybd = [None, None]
    m_o1sp = {}
    m_xmtsp = {}
    qeDp = [qeD, qeD2]
    st_og = [Stream(p, f"og{i}", "gp") for i in range(2)]

    def d_tail(jt):
        pt = jt % 2

        def trm(e, pt=pt):
            for c in range(NCH):
                mm = e.transpose(PS[:, 2048 + c * 128:2048 + (c + 1) * 128],
                                 qeDp[pt][:, c * 128:(c + 1) * 128], identf[:])
            return mm
        mtm = p.op("pe", trm, waits=[m_xm_act[pt], m_xmT_ev[0], m_xmT_act[0]], inc=SPE)
        qe_rd[qeDp[pt].name] = mtm
        w_mb = [mtm]
        if jt >= 2:
            w_mb.append((f"sm{pt}", 16 * st_m[pt].n))
        mmb = p.op("act", lambda e, pp=pt: e.activation(bfc[pp][:], PS[:, 2048:3072],
                                                        AF.Identity), waits=w_mb, inc=SACT)
        m_xmT_act[0] = mmb
        mm32 = p.op("dve", lambda e: e.tensor_copy(xm32[:], PS[:, 2048:3072]),
                    waits=[mtm, mmb, m_gate_ev[0]], inc=SDVE)
        m_xmT_ev[0] = mm32
        m_xmsp[jt] = st_m[pt].dma(
            xm_sp.rearrange("c d t -> d c t")[:, :, jt * 128:(jt + 1) * 128],
            bfc[pt][:].rearrange("p (c t) -> p c t", c=NCH), waits=[mmb])
        w_xt = [m_xm_act[pt]]
        if jt >= 2:
            w_xt.append((f"st{pt}", 16 * st_t[pt].n))
        mxtb = p.op("act", lambda e, pp=pt: e.activation(bfd[pp][:], qeDp[pt][:], AF.Identity),
                    waits=w_xt, inc=SACT)
        m_xmtsp[jt] = st_t[pt].dma(xmt_sp[jt * 128:(jt + 1) * 128, :], bfd[pt][:], waits=[mxtb])

        def gmm(e):
            for c in range(NCH):
                mm = e.matmul(PS[:, 3072:3104], xm32[:, c * 128:(c + 1) * 128],
                              gwTs[:, c * 32:(c + 1) * 32], start=(c == 0), stop=(c == NCH - 1))
            return mm
        mg = p.op("pe", gmm, waits=[mm32, m_gw, m_gate_ev[0]], inc=SPE)
        m_gate_ev[0] = p.op("dve", lambda e, jt=jt: e.tensor_copy(rsc1[:, jt * 32:(jt + 1) * 32],
                                                                  PS[:, 3072:3104]),
                            waits=[mg], inc=SDVE)

    m_gate_ev = [None]
    for j in range(NTB_OWN):
        par = j % 2
        mr3 = st_3[par].dma(xt[par][:], x3_sp[j * 128:(j + 1) * 128, :],
                            waits=[m_xm_act[par], m_o1sp.get(j - 2)])
        mrx = st_x[par].dma(yt[par][:], x_in[j * 128:(j + 1) * 128, :],
                            waits=[fC_pe] if j < 2 else [m_ybd[par]])

        def tr3(e, pp=par):
            for c in range(NCH):
                mm = e.transpose(PS[:, c * 128:(c + 1) * 128],
                                 xt[pp][:, c * 128:(c + 1) * 128], identf[:])
            return mm
        w_t3 = [mr3, m_x3T_ev, C_IDENTF]
        if j == 0:
            w_t3 += [fC_act, fC_dve]
        mt3 = p.op("pe", tr3, waits=w_t3, inc=SPE)
        mev3 = p.op("act", lambda e: e.activation(bfa[:], PS[:, 0:1024], AF.Identity),
                    waits=[mt3], inc=SACT)
        m_x3T_ev = mev3

        def wdmm(e):
            for c in range(NCH):
                lhs = bfa[:, c * 128:(c + 1) * 128]
                e.matmul(bank(2), lhs, wd[:, c * D:c * D + 512],
                         start=(c == 0), stop=(c == NCH - 1))
                mm = e.matmul(bank(3), lhs, wd[:, c * D + 512:(c + 1) * D],
                              start=(c == 0), stop=(c == NCH - 1))
            return mm
        w_wd = [mev3, m_wd, m_wd_dve]
        if j == 0:
            w_wd += [fC_dve]
        if dsub < 1:
            m_x3T_ev = mev3
            m_o1_dve[par] = mr3
            continue
        mwd = p.op("pe", wdmm, waits=w_wd, inc=SPE)
        mxb = p.op("dve", lambda e, o=yb[par][:], i=yt[par][:]:
                   e.tensor_tensor(o, i, bd_rep[:], ALU.add), waits=[mrx, m_bd, mr3], inc=SDVE)
        m_ybd[par] = mxb
        mo1 = p.op("dve", lambda e, o=xt[par][:], i=yb[par][:]:
                   e.tensor_tensor(o, i, bank(2, 2), ALU.add), waits=[mwd, mxb], inc=SDVE)
        m_wd_dve = mo1
        m_o1_dve[par] = mo1
        m_o1sp[j] = st_og[par].dma(o1_sp[j * 128:(j + 1) * 128, :], xt[par][:], waits=[mo1])
        if j >= 1:
            d_tail(j - 1)
        rstd3, bv3, _, md4d, _sD, _vD, _mD = ln_stats(xt[par][:], par * 8, mo1, [qe, qeB][par])
        mxm = p.op("act", lambda e, s=xt[par][:], r=rstd3, b=bv3, o=qeDp[par][:]:
                   e.activation(o, s, AF.Identity, bias=b, scale=r),
                   waits=[md4d, qe_rd.get(qeDp[par].name)], inc=SACT)
        m_xm_act[par] = mxm
    d_tail(NTB_OWN - 1)

    if dsub < 4:
        return finish()
    fC_pe_pre = fC_pe
    fD_pe_pre = (SPE, p.ctr[SPE])
    fD_act_pre = (SACT, p.ctr[SACT])
    fD_dve_pre = (SDVE, p.ctr[SDVE])
    st_xq = [Stream(p, f"xq{i}", "sy") for i in range(2)]
    m_xmq_pre = {}
    for Q in range(min(2, NQ)):
        m_xmq_pre[Q] = st_xq[Q % 2].dma(
            xmq[Q % 2][:].rearrange("p (c t) -> p c t", c=NCH),
            xm_sp[:, :, Q * QTB * 128:(Q + 1) * QTB * 128].rearrange("c d t -> d c t"),
            waits=[fC_pe_pre, m_xmsp[Q * QTB + QTB - 2], m_xmsp[Q * QTB + QTB - 1]])
    NT = NTB_OWN
    L = NT * 32
    G = NT * 8
    sc3 = rsc2[:, 0:L].rearrange("p (t e) -> p t e", t=NT)
    sfc3 = combine[:, 0:L].rearrange("p (t e) -> p t e", t=NT)
    sfc4 = combine[:, 0:L].rearrange("p (t g s) -> p t g s", t=NT, g=8)
    gsc = rsc1[:, 0:G].rearrange("p (t g) -> p t g", t=NT)
    wkg = xm32[:, 0:G].rearrange("p (t g) -> p t g", t=NT)
    mskg = xm32[:, G:2 * G].rearrange("p (t g) -> p t g", t=NT)
    wk6 = qe[:, 0:L].rearrange("p (t e) -> p t e", t=NT)
    msk6 = xm32[:, 0:L].rearrange("p (t e) -> p t e", t=NT)

    def ncol(idx):
        return rsc1[:, idx * NT:(idx + 1) * NT]

    rt = [qe_rd.get(qe.name), qe_rd.get(qeB.name)]

    def rop(fn, extra=()):
        m = p.op("dve", fn, waits=rt + list(extra), inc=SDVE)
        rt.clear(); rt.append(m)
        return m

    mr0 = p.op("act", lambda e: e.activation(rsc2[:, 0:L], rsc1[:, 0:L], AF.Sigmoid),
               waits=[m_gate_ev[0]], inc=SACT)
    rop(lambda e: e.tensor_tensor(
        sfc3, sc3, esc[:].unsqueeze(1).broadcast_to([128, NT, 32]), ALU.add),
        extra=[mr0, C_ESC])
    rop(lambda e: e.reduce_max(gsc, sfc4, axis=AX.X))
    rop(lambda e: e.tensor_copy(xm32[:, 0:G], rsc1[:, 0:G]))
    for r in range(4):
        mcol = ncol(8 + r)
        rop(lambda e, o=mcol: e.reduce_max(o, wkg, axis=AX.X))
        rop(lambda e, m=mcol: e.tensor_tensor(
            mskg, wkg, m.rearrange("p (t o) -> p t o", t=NT).broadcast_to([128, NT, 8]),
            ALU.is_ge))
        rop(lambda e: e.scalar_tensor_tensor(
            xm32[:, 0:G], xm32[:, G:2 * G], -1e9, xm32[:, 0:G], ALU.mult, ALU.add))
    th4 = ncol(11)
    rop(lambda e: e.tensor_tensor(
        mskg, gsc, th4.rearrange("p (t o) -> p t o", t=NT).broadcast_to([128, NT, 8]),
        ALU.is_ge))
    gm4 = xm32[:, G:2 * G].rearrange("p (t g) -> p t g", t=NT).unsqueeze(3) \
        .broadcast_to([128, NT, 8, 4])
    rop(lambda e: e.tensor_tensor(sfc4, sfc4, gm4, ALU.mult))
    rop(lambda e: e.tensor_copy(qe[:, 0:L], combine[:, 0:L]))
    for r in range(6):
        mcol = ncol(12 + r)
        rop(lambda e, o=mcol: e.reduce_max(o, wk6, axis=AX.X))
        rop(lambda e, m=mcol: e.tensor_tensor(
            msk6, wk6, m.rearrange("p (t o) -> p t o", t=NT).broadcast_to([128, NT, 32]),
            ALU.is_ge))
        rop(lambda e, r=r: e.tensor_copy(mks[:, r * L:(r + 1) * L], xm32[:, 0:L]))
        rop(lambda e: e.scalar_tensor_tensor(
            qe[:, 0:L], xm32[:, 0:L], -1e9, qe[:, 0:L], ALU.mult, ALU.add))
    th6 = ncol(17)
    rop(lambda e: e.tensor_tensor(
        msk6, sfc3, th6.rearrange("p (t o) -> p t o", t=NT).broadcast_to([128, NT, 32]),
        ALU.is_ge))
    rop(lambda e: e.tensor_tensor(qe[:, 0:L], xm32[:, 0:L], rsc2[:, 0:L], ALU.mult))
    rop(lambda e: e.reduce_sum(ncol(18).rearrange("p (t o) -> p t o", t=NT), wk6, axis=AX.X))
    rop(lambda e: e.tensor_scalar(ncol(19), ncol(18), 1e-20, None, ALU.add))
    rop(lambda e: e.reciprocal(ncol(20), ncol(19)))
    rop(lambda e: e.tensor_scalar(ncol(21), ncol(20), 2.5, None, ALU.mult))
    cmb3 = combine[:].rearrange("p (t e) -> p t e", e=NE)
    rop(lambda e: e.tensor_tensor(
        cmb3[:, :, 0:32], wk6,
        ncol(21).rearrange("p (t o) -> p t o", t=NT).broadcast_to([128, NT, 32]),
        ALU.mult))
    mcmb = rop(lambda e: e.memset(cmb3[:, :, 32:34], 1.0))
    qe_rd[qe.name] = mcmb
    if dbg:
        st_c.dma(cmb_db, combine[:], waits=[mcmb])

    m_lt = st_c.dma(lt128[:], lt128_in, waits=[fC_pe_pre])
    m_lt2 = st_c.dma(lt32[:], lt32_in)
    m_eo = st_c.dma(eoffm1[:], eoffm1_in)
    m_selbf = rop(lambda e: e.tensor_copy(qsf[:, 0:L], xm32[:, 0:L]))

    def rankmm(e):
        for tb in range(NT):
            mm = e.matmul(PS[:, 3072 + tb * 32:3072 + (tb + 1) * 32], lt128[:],
                          qsf[:, tb * 32:(tb + 1) * 32],
                          start=(tb % 16 == 0), stop=(tb % 16 == 15 or tb == NT - 1),
                          skip_group_check=True)
        return mm
    mrk = p.op("pe", rankmm, waits=[m_selbf, m_lt, fD_act_pre, fD_dve_pre], inc=SPE)
    mrk_ev = p.op("dve", lambda e: e.tensor_copy(rsc2[:, 0:L], PS[:, 3072:3072 + NT * 32]),
                  waits=[mrk], inc=SDVE)
    m_cr = st_c.dma(crow_dram[0:1, :], rsc2[127:128, 0:L], waits=[mrk_ev])
    m_ct = st_sc.dma(ctb[0:NT, :], crow_dram.rearrange("o (t e) -> (o t) e", t=NT),
                     waits=[m_cr])
    mbm = p.op("pe", lambda e: e.matmul(PS[0:NT, 3072:3104], lt32[0:NT, 0:NT], ctb[0:NT, :],
                                        start=True, stop=True),
               waits=[m_ct, m_lt2, mrk_ev], inc=SPE)
    mbe = p.op("dve", lambda e: e.tensor_copy(bsb[0:NT, :], PS[0:NT, 3072:3104]),
               waits=[mbm], inc=SDVE)
    m_b1 = st_c.dma(brow_dram, bsb[0:NT, :], waits=[mbe])
    m_br = st_c.dma(yb[1][0:1, 0:L], brow_dram.rearrange("t e -> (t e)").unsqueeze(0),
                    waits=[m_b1])
    monef = p.op("dve", lambda e: e.memset(onesf[:], 1.0), inc=SDVE)

    def bbc(e):
        for of in range(0, L, 512):
            w = min(512, L - of)
            mm = e.matmul(PS[:, 3072 + of:3072 + of + w], onesf[0:1, :], yb[1][0:1, of:of + w],
                          start=True, stop=True)
        return mm
    mbb = p.op("pe", bbc, waits=[m_br, monef, mrk_ev], inc=SPE)
    mbbe = p.op("dve", lambda e: e.tensor_copy(rsc1[:, 0:L], PS[:, 3072:3072 + L]),
                waits=[mbb], inc=SDVE)
    rt2 = [mbbe, mrk_ev]

    def sop(fn, extra=()):
        rt2[0] = p.op("dve", fn, waits=rt2 + list(extra), inc=SDVE)
        del rt2[1:]
        return rt2[0]

    sop(lambda e: e.tensor_tensor(rsc2[:, 0:L], rsc2[:, 0:L], rsc1[:, 0:L], ALU.add))
    sop(lambda e: e.tensor_scalar(qe[:, 0:L], rsc2[:, 0:L], CAP + 0.5, None, ALU.is_gt))
    sop(lambda e: e.tensor_tensor(
        rsc2[:, 0:L].rearrange("p (t e) -> p t e", t=NT),
        rsc2[:, 0:L].rearrange("p (t e) -> p t e", t=NT),
        eoffm1[:].unsqueeze(1).broadcast_to([128, NT, 32]), ALU.add), extra=[m_eo])
    sop(lambda e: e.scalar_tensor_tensor(rsc2[:, 0:L], qe[:, 0:L], 1e9, rsc2[:, 0:L],
                                         ALU.mult, ALU.add))
    sop(lambda e: e.tensor_scalar(qeB[:, 0:L], xm32[:, 0:L], -1e9, 1e9, ALU.mult, ALU.add))
    sop(lambda e: e.tensor_tensor(rsc2[:, 0:L], rsc2[:, 0:L], qeB[:, 0:L], ALU.add))
    sop(lambda e: e.tensor_scalar(rsc2[:, 0:L], rsc2[:, 0:L], float(32 * CAP), None, ALU.min))
    for jj in range(6):
        mkj = mks[:, jj * L:(jj + 1) * L].rearrange("p (t e) -> p t e", t=NT)
        sop(lambda e, mkj=mkj: e.tensor_tensor(
            qe[:, 0:L].rearrange("p (t e) -> p t e", t=NT),
            rsc2[:, 0:L].rearrange("p (t e) -> p t e", t=NT), mkj, ALU.mult))
        sop(lambda e: e.reduce_sum(yt[1][0:128, 0:NT],
                                   qe[:, 0:L].rearrange("p (t e) -> p t e", t=NT), axis=AX.X))
        sop(lambda e, jj=jj: e.tensor_copy(posi[:, jj * NT:(jj + 1) * NT], yt[1][0:128, 0:NT]))
        sop(lambda e, mkj=mkj: e.tensor_tensor(
            qe[:, 0:L].rearrange("p (t e) -> p t e", t=NT),
            combine[:, 0:NT * NE].rearrange("p (t e) -> p t e", t=NT)[:, :, 0:32],
            mkj, ALU.mult))
        sop(lambda e, jj=jj: e.reduce_sum(wgtj[:, jj * NT:(jj + 1) * NT],
                                          qe[:, 0:L].rearrange("p (t e) -> p t e", t=NT),
                                          axis=AX.X))
    m_pos = rt2[0]
    if dbg:
        st_c.dma(posi_db, posi[:, 0:6 * NT], waits=[m_pos])
        st_c.dma(wgtj_db, wgtj[:, 0:6 * NT], waits=[m_pos])
    m_egu = {}
    m_edn = {}
    for ei, e_ in enumerate([32, 33]):
        m_egu[(0, ei)] = st_e[ei % 2].dma(
            egu[ei % 2][:].rearrange("p (c f) -> p c f", c=NCH), _r3(egu_bf[e_]),
            waits=[fC_pe_pre, m_pcg[e_]])
        m_edn[(0, ei)] = st_n[ei % 2].dma(
            edn[ei % 2][:].rearrange("p (c f) -> p c f", c=4), _r3(edn_bf[e_]),
            waits=[fC_pe_pre, m_pcd[e_]])
    mzr = p.op("dve", lambda e: e.memset(obf[0][0:1, :], 0.0), waits=[m_cr], inc=SDVE)
    st_c.dma(outp_dram[32 * CAP:32 * CAP + 1, :], obf[0][0:1, :], waits=[mzr])
    qfm4 = qfm + qfm4e
    msc = {}
    for tb in range(NT):
        par4 = tb % 4
        mrl = st_t[tb % 2].dma(qfm4[par4][:], xmt_sp[tb * 128:(tb + 1) * 128, :],
                               waits=msc.get(tb - 4, [None]))
        msc[tb] = []
        for jj in range(6):
            sq = st_s4[(tb * 6 + jj) % 4]
            msc[tb].append(p.op("gp", lambda e, par4=par4, jj=jj, tb=tb: e.indirect_dma_start(
                out=xg_dram, out_offset=IndirectOffsetOnAxis(
                    ap=posi[:, jj * NT + tb:jj * NT + tb + 1], axis=0),
                in_=qfm4[par4][:], in_offset=None),
                waits=[mrl, m_pos, (sq.name, 16 * sq.n)], inc=sq.name, inc_n=16))
            sq.n += 1
    f_scat4 = [(s.name, 16 * s.n) for s in st_s4]

    fD_pe = (SPE, p.ctr[SPE])
    fD_act = (SACT, p.ctr[SACT])
    nm = [st_m[0].n, st_m[1].n]
    if upto == 'D':
        return finish()

    DE = [32, 33]
    m_xmq = [None, None]
    m_act_done = {}
    m_tt = {}
    m_evac = {}
    m_acc = {}
    m_pe_gu = {}
    m_pe_tr = {}
    m_pe_dn = {}
    m_final = []
    m_fin_t = {}
    m_out_prev = {}
    SQ = len(DE) * QTB

    m_xmq_sub = {}

    def emit_xmq_sub(Q):
        qpar = Q % 2
        for h in range(QTB // 2):
            t0 = 2 * h
            m_xmq_sub[(qpar, h)] = st_xq[h % 2].dma(
                xmq[qpar][:].rearrange("p (c t) -> p c t", c=NCH, t=QTB * 128)
                [:, :, t0 * 128:(t0 + 2) * 128],
                xm_sp[:, :, (Q * QTB + t0) * 128:(Q * QTB + t0 + 2) * 128]
                .rearrange("c d t -> d c t"),
                waits=[m_xmsp[Q * QTB + t0], m_xmsp[Q * QTB + t0 + 1],
                       m_pe_gu[(Q - 2, len(DE) - 1, min(t0 + 1, QTB - 1))]])
        m_xmq[qpar] = m_xmq_sub[(qpar, QTB // 2 - 1)]

    for Q in range(NQ):
        qpar = Q % 2
        if Q < 2:
            m_xmq[qpar] = m_xmq_pre[Q]
            for h in range(QTB // 2):
                m_xmq_sub[(qpar, h)] = m_xmq_pre[Q]

        def xq(sp):
            if sp >= 2:
                return (Q, sp - 2)
            return (Q - 1, SQ - 2 + sp) if Q > 0 else None

        for s in range(SQ + 2):
            if s < SQ:
                ei, t = divmod(s, QTB)
                e_ = DE[ei]
                par = s % 2
                epar = ei % 2
                gb = (0, 1) if par == 0 else (2, 3)
                def gumm(e, t=t, gb=gb, epar=epar, qpar=qpar):
                    for c in range(NCH):
                        lhs = xmq[qpar][:, (c * QTB + t) * 128:(c * QTB + t) * 128 + 128]
                        e.matmul(bank(gb[0]), lhs, egu[epar][:, c * D:c * D + 512],
                                 start=(c == 0), stop=(c == NCH - 1))
                        mm = e.matmul(bank(gb[1]), lhs, egu[epar][:, c * D + 512:(c + 1) * D],
                                      start=(c == 0), stop=(c == NCH - 1))
                    return mm
                w_gu = [m_egu[(0, ei)], m_edn[(0, ei)], m_xmq_sub[(qpar, t // 2)]]
                if Q == 0 and s < 2:
                    w_gu += [fD_pe, fD_act, fD_dve_pre]
                if xq(s):
                    w_gu.append(m_act_done[xq(s)])
                m_pe_gu[(Q, ei, t)] = p.op("pe", gumm, waits=w_gu, inc=SPE)
                w_a = [m_pe_gu[(Q, ei, t)]]
                if xq(s):
                    w_a.append(m_tt[xq(s)])
                if Q == 0 and s < 2:
                    w_a += [(f"sm0", 16 * nm[0]), (f"sm1", 16 * nm[1])]
                p.op("act", lambda e, gb=gb, par=par: e.activation(
                    bfc[par][:, 0:512], bank(gb[0]), AF.Silu), waits=w_a, inc=SACT)
                m_act_done[(Q, s)] = p.op("act", lambda e, gb=gb, par=par:
                                          e.activation(bfc[par][:, 512:1024], bank(gb[1]),
                                                       AF.Identity),
                                          inc=SACT)
                w_tt = [m_act_done[(Q, s)]]
                if xq(s):
                    w_tt.append(m_pe_tr[xq(s)])
                m_tt[(Q, s)] = p.op("dve", lambda e, par=par: e.tensor_mul(
                    bfd[par][:, 0:512], bfc[par][:, 0:512], bfc[par][:, 512:1024]),
                    waits=w_tt, inc=SDVE)
            if 0 <= s - 1 < SQ:
                sp = s - 1
                par = sp % 2
                tb_ = 4 + par

                def trq(e, par=par, tb_=tb_):
                    for jj2 in range(4):
                        mm = e.transpose(bankb(tb_)[:, jj2 * 128:(jj2 + 1) * 128],
                                         bfd[par][:, jj2 * 128:(jj2 + 1) * 128], identb[:])
                    return mm
                w_tr = [m_tt[(Q, sp)]]
                if Q == 0 and sp < 2:
                    w_tr.append(fD_dve_pre)
                if xq(sp):
                    w_tr.append(m_evac[xq(sp)])
                m_pe_tr[(Q, sp)] = p.op("pe", trq, waits=w_tr, inc=SPE)
                w_ev = [m_pe_tr[(Q, sp)]]
                if xq(sp):
                    w_ev.append(m_pe_dn[xq(sp)])
                m_evac[(Q, sp)] = p.op("dve", lambda e, par=par, tb_=tb_: e.tensor_copy(
                    bfd[par][:, 512:1024], bankb(tb_)[:, 0:512]), waits=w_ev, inc=SDVE)
            if 0 <= s - 2 < SQ:
                sp = s - 2
                ei, t = divmod(sp, QTB)
                par = sp % 2
                epar = ei % 2

                def dnmm(e, par=par, epar=epar):
                    for c in range(4):
                        lhs = bfd[par][:, 512 + c * 128:512 + (c + 1) * 128]
                        e.matmul(bank(6), lhs, edn[epar][:, c * D:c * D + 512],
                                 start=(c == 0), stop=(c == 3))
                        mm = e.matmul(bank(7), lhs, edn[epar][:, c * D + 512:(c + 1) * D],
                                      start=(c == 0), stop=(c == 3))
                    return mm
                w_dn = [m_evac[(Q, sp)]]
                if Q == 0 and sp == 0:
                    w_dn.append(mbbe)
                if sp >= 1:
                    w_dn.append(m_acc[(Q, sp - 1)])
                elif Q > 0:
                    w_dn.append(m_acc[(Q - 1, SQ - 1)])
                m_pe_dn[(Q, sp)] = p.op("pe", dnmm, waits=w_dn, inc=SPE)
                if ei == 0:
                    w_ac = [m_pe_dn[(Q, sp)], m_fin_t.get(t)]
                    m_acc[(Q, sp)] = p.op("dve", lambda e, t=t: e.tensor_copy(
                        acc[:, t * D:(t + 1) * D], bank(6, 2)), waits=w_ac, inc=SDVE)
                else:
                    m_acc[(Q, sp)] = p.op("dve", lambda e, t=t: e.tensor_tensor(
                        acc[:, t * D:(t + 1) * D], acc[:, t * D:(t + 1) * D], bank(6, 2),
                        ALU.add), waits=[m_pe_dn[(Q, sp)]], inc=SDVE)
                    j = Q * QTB + t
                    par_j = j % 2
                    mro = st_o[par_j].dma(xt[par_j][:], o1_sp[j * 128:(j + 1) * 128, :],
                                          waits=[m_out_prev.get(par_j), m_o1sp[j]])
                    mfin = p.op("dve", lambda e, t=t, par_j=par_j: e.tensor_tensor(
                        yt[par_j][:], xt[par_j][:], acc[:, t * D:(t + 1) * D], ALU.add),
                        waits=[mro, m_acc[(Q, sp)],
                               (f"sy{par_j}", 16 * st_y[par_j].n)],
                        inc=SDVE)
                    m_final.append(mfin)
                    m_fin_t[t] = mfin
                    m_out_prev[par_j] = mfin
                    st_y[par_j].dma(o1_sp[j * 128:(j + 1) * 128, :], yt[par_j][:],
                                    waits=[mfin])

        if Q + 2 < NQ:
            emit_xmq_sub(Q + 2)
    f_shared = [(f"sy{i}", 16 * st_y[i].n) for i in range(2)]

    xft = [qsf, bfa]
    m_xg = {}
    m_itr = {}
    m_iev = {}
    m_rgu = {}
    m_rtr = {}
    m_rdn = {}
    m_ract = {}
    m_rtt = {}
    m_oev = {}
    m_ost = {}
    SR = 32 * TPE
    st_xg = [Stream(p, f"sx{i}g", "sy") for i in range(4)]
    st_op = [Stream(p, f"so{i}p", "gp") for i in range(4)]

    def egu_load(e_, waits):
        ep = e_ % 2
        ma = st_e[2 * ep].dma(
            egu[ep][:, 0:4 * D].rearrange("p (c f) -> p c f", c=4), _r3(egu_bf[e_][0:512]),
            waits=waits + [m_pcg[e_]])
        mb = st_e[2 * ep + 1].dma(
            egu[ep][:, 4 * D:8 * D].rearrange("p (c f) -> p c f", c=4), _r3(egu_bf[e_][512:1024]),
            waits=waits + [m_pcg[e_]])
        return [ma, mb]

    for e_ in range(2):
        m_egu[("r", e_)] = egu_load(e_, [m_pe_gu[(NQ - 1, e_, QTB - 1)]])
        m_edn[("r", e_)] = st_n[e_ % 2].dma(
            edn[e_ % 2][:].rearrange("p (c f) -> p c f", c=4), _r3(edn_bf[e_]),
            waits=[m_pe_dn[(NQ - 1, SQ - 1)], m_pcd[e_]])

    for k in range(SR + 3):
        if k >= 1 and (k - 1) % TPE == 0:
            eg = (k - 1) // TPE + 1
            if 2 <= eg < 32:
                m_egu[("r", eg)] = egu_load(eg, [m_rgu[k - 2]])
        if k < SR:
            e_, kt = divmod(k, TPE)
            par = k % 2
            par4 = k % 4
            epar = e_ % 2
            m_xg[par4] = st_xg[par4].dma(
                qfm4[par4][:], xg_dram[(e_ * CAP + kt * 128):(e_ * CAP + kt * 128 + 128), :],
                waits=f_scat4 + [m_itr.get(k - 4)])
            w_it = [m_xg[par4], m_iev.get(k - 1)]
            if k == 0:
                w_it += [m_acc[(NQ - 1, SQ - 1)], m_act_done[(NQ - 1, SQ - 1)],
                         m_evac[(NQ - 1, SQ - 1)]]
            def itr(e, par4=par4):
                for c in range(NCH):
                    mm = e.transpose(bankb(4)[:, c * 128:(c + 1) * 128],
                                     qfm4[par4][:, c * 128:(c + 1) * 128], identb[:])
                return mm
            m_itr[k] = p.op("pe", itr, waits=w_it, inc=SPE)
            m_iev[k] = p.op("dve", lambda e, par=par: e.tensor_copy(xft[par][:], bankb(4)),
                            waits=[m_itr[k], m_rgu.get(k - 2)], inc=SDVE)
        if 0 <= k - 1 < SR:
            kk = k - 1
            e_, kt = divmod(kk, TPE)
            par = kk % 2
            epar = e_ % 2
            gb = (0, 1) if par == 0 else (2, 3)

            def rgu(e, par=par, epar=epar, gb=gb):
                for c in range(NCH):
                    lhs = xft[par][:, c * 128:(c + 1) * 128]
                    e.matmul(bank(gb[0]), lhs, egu[epar][:, c * D:c * D + 512],
                             start=(c == 0), stop=(c == NCH - 1))
                    mm = e.matmul(bank(gb[1]), lhs, egu[epar][:, c * D + 512:(c + 1) * D],
                                  start=(c == 0), stop=(c == NCH - 1))
                return mm
            w_rg = [m_iev[kk], m_ract.get(kk - 2)] + m_egu[("r", e_)]
            m_rgu[kk] = p.op("pe", rgu, waits=w_rg, inc=SPE)
            p.op("act", lambda e, gb=gb, par=par: e.activation(
                bfc[par][:, 0:512], bank(gb[0]), AF.Silu), waits=[m_rgu[kk]], inc=SACT)
            m_ract[kk] = p.op("act", lambda e, gb=gb, par=par: e.activation(
                bfc[par][:, 512:1024], bank(gb[1]), AF.Identity), inc=SACT)
            m_rtt[kk] = p.op("dve", lambda e, par=par: e.tensor_mul(
                bfd[par][:, 0:512], bfc[par][:, 0:512], bfc[par][:, 512:1024]),
                waits=[m_ract[kk], m_rtr.get(kk - 2)], inc=SDVE)
        if 0 <= k - 2 < SR:
            kk = k - 2
            par = kk % 2

            def rtr(e, par=par):
                for jj2 in range(4):
                    mm = e.transpose(bankb(5)[:, jj2 * 128:(jj2 + 1) * 128],
                                     bfd[par][:, jj2 * 128:(jj2 + 1) * 128], identb[:])
                return mm
            m_rtr[kk] = p.op("pe", rtr, waits=[m_rtt[kk], m_evac.get(("r", kk - 1))], inc=SPE)
            m_evac[("r", kk)] = p.op("dve", lambda e, par=par: e.tensor_copy(
                bfd[par][:, 512:1024], bankb(5)[:, 0:512]),
                waits=[m_rtr[kk], m_rdn.get(kk - 2)], inc=SDVE)
        if 0 <= k - 3 < SR:
            kk = k - 3
            e_, kt = divmod(kk, TPE)
            par = kk % 2
            epar = e_ % 2

            def rdn(e, par=par, epar=epar):
                for c in range(4):
                    lhs = bfd[par][:, 512 + c * 128:512 + (c + 1) * 128]
                    e.matmul(bank(6), lhs, edn[epar][:, c * D:c * D + 512],
                             start=(c == 0), stop=(c == 3))
                    mm = e.matmul(bank(7), lhs, edn[epar][:, c * D + 512:(c + 1) * D],
                                  start=(c == 0), stop=(c == 3))
                return mm
            w_rd = [m_evac[("r", kk)], m_edn[("r", e_)], m_oev.get(kk - 1)]
            m_rdn[kk] = p.op("pe", rdn, waits=w_rd, inc=SPE)
            k4 = kk % 4
            w_oe = [m_rdn[kk]]
            if kk >= 4:
                w_oe.append((st_op[k4].name, 16 * st_op[k4].n))
            m_oev[kk] = p.op("dve", lambda e, k4=k4: e.tensor_copy(obf[k4][:], bank(6, 2)),
                             waits=w_oe, inc=SDVE)
            e2_, kt2 = divmod(kk, TPE)
            m_ost[kk] = st_op[k4].dma(
                outp_dram[(e2_ * CAP + kt2 * 128):(e2_ * CAP + kt2 * 128 + 128), :],
                obf[k4][:], waits=[m_oev[kk]])
            if kt2 == TPE - 1 and e2_ + 2 < 32:
                en = e2_ + 2
                m_edn[("r", en)] = st_n[en % 2].dma(
                    edn[en % 2][:].rearrange("p (c f) -> p c f", c=4), _r3(edn_bf[en]),
                    waits=[m_rdn[kk], m_pcd[en]])
    f_routed = [(s.name, 16 * s.n) for s in st_op]

    gq = st_gt + st_s4 + st_pc + st_e + st_n + st_og
    gbuf = ybb + obf + gbt
    NG = 24
    m_g_prev = [None] * NG
    g_used = [False] * NG
    ring = [xt[0], xt[1], yt[0], yt[1]]
    rq = st_o + st_3
    m_fin_prev = [None] * 4
    for j in range(NTB_OWN):
        r4 = j % 4
        w_ro = [m_fin_prev[r4]] + f_shared
        if j >= 4:
            w_ro.append((f"sw{j % 2}", 16 * (j // 2 - 1)))
        mro = rq[r4].dma(ring[r4][:], o1_sp[j * 128:(j + 1) * 128, :], waits=w_ro)
        macc = mro
        for jj in range(6):
            gpar = (j * 6 + jj) % NG
            sq = gq[gpar]
            w_g = [m_pos, m_g_prev[gpar], (sq.name, 16 * sq.n) if sq.n else None]
            if not g_used[gpar]:
                w_g += f_routed
                g_used[gpar] = True
            mg_ = p.op("gp", lambda e, j=j, jj=jj, gpar=gpar: e.indirect_dma_start(
                out=gbuf[gpar][:], out_offset=None,
                in_=outp_dram,
                in_offset=IndirectOffsetOnAxis(ap=posi[:, jj * NT + j:jj * NT + j + 1], axis=0)),
                waits=w_g, inc=sq.name, inc_n=16)
            sq.n += 1
            macc = p.op("dve", lambda e, j=j, jj=jj, gpar=gpar, r4=r4:
                        e.scalar_tensor_tensor(ring[r4][:], gbuf[gpar][:],
                                               wgtj[:, jj * NT + j:jj * NT + j + 1],
                                               ring[r4][:], ALU.mult, ALU.add),
                        waits=[mg_, macc], inc=SDVE)
            m_g_prev[gpar] = macc
        m_fin_prev[r4] = macc
        st_w[j % 2].dma(out_dram[j * 128:(j + 1) * 128, :], ring[r4][:], waits=[macc])

    p.op("sy", None, waits=[(f"sw{i}", 16 * st_w[i].n) for i in range(2)])

    import contextlib
    stack = contextlib.ExitStack()
    sems = {name: stack.enter_context(nc.semaphore(name)) for name in sorted(p.sem_names)}
    with nc.Block() as block:
        p.emit(block, sems)
    return nc


_BF = ml_dtypes.bfloat16


def _stage(inputs):
    x = np.asarray(inputs["L0_attn_x"])
    f32 = np.float32

    def c_(a):
        return np.ascontiguousarray(np.asarray(a, f32))

    def cols(v, n):
        return np.ascontiguousarray(np.asarray(v, f32).reshape(n, 128).T)

    def rep(v):
        v = np.asarray(v, f32)
        return np.ascontiguousarray(np.broadcast_to(v[None, :], (128, v.shape[0])))

    gwT = c_(np.asarray(inputs["L0_ffn_gate_weight"], f32).T)
    sgT = np.asarray(inputs["L0_ffn_shared_gate_w"], f32).T
    suT = np.asarray(inputs["L0_ffn_shared_up_w"], f32).T
    sdT = np.asarray(inputs["L0_ffn_shared_down_w"], f32).T
    eguT = np.asarray(inputs["L0_ffn_experts_gate_up"], f32).transpose(0, 2, 1)
    ednT = np.asarray(inputs["L0_ffn_experts_down"], f32).transpose(0, 2, 1)
    sh32 = np.concatenate([sgT[:, 0:512], suT[:, 0:512]], axis=1)[None]
    sh33 = np.concatenate([sgT[:, 512:1024], suT[:, 512:1024]], axis=1)[None]
    egu = np.ascontiguousarray(np.concatenate([eguT, sh32, sh33], axis=0))
    edn = np.ascontiguousarray(np.concatenate([ednT, sdT[None, 0:512], sdT[None, 512:1024]],
                                              axis=0))
    common = dict(
        wq=c_(inputs["L0_attn_wq"]), wk=c_(inputs["L0_attn_wk"]), wv=c_(inputs["L0_attn_wv"]),
        wo=c_(inputs["L0_attn_wo"]), wd=c_(inputs["L0_attn_w_d"]),
        ff1=c_(inputs["L0_attn_w_ff1"]), ff2=c_(inputs["L0_attn_w_ff2"]),
        g1=cols(inputs["L0_attn_ln1_g"], 8), b1=cols(inputs["L0_attn_ln1_b"], 8),
        g2=cols(inputs["L0_attn_ln2_g"], 8), b2=cols(inputs["L0_attn_ln2_b"], 8),
        bff1=cols(inputs["L0_attn_b_ff1"], 32),
        bo_rep=rep(inputs["L0_attn_bo"]),
        bff2_rep=rep(inputs["L0_attn_b_ff2"]),
        bd_rep=rep(inputs["L0_attn_b_d"]),
        esc_rep=rep(inputs["L0_ffn_e_score_correction_bias"]),
        gwT=gwT, egu=egu, edn=edn,
        identb=np.eye(128, dtype=_BF), identf=np.eye(128, dtype=np.float32),
        lt128=np.triu(np.ones((128, 128), _BF)),
        lt32=np.triu(np.ones((32, 32), _BF), 1),
        eoffm1=rep(np.arange(32, dtype=np.float32) * CAP - 1),
    )
    in_maps = []
    for core in range(8):
        b, qtr = divmod(core, 4)
        xb = np.asarray(x[b], f32)
        xc = np.ascontiguousarray(np.concatenate(
            [xb[qtr * 4096:(qtr + 1) * 4096], xb[:qtr * 4096], xb[(qtr + 1) * 4096:]], axis=0))
        in_maps.append(dict(common, x=xc))
    return in_maps


def kernel(**inputs):
    in_maps = _stage(inputs)
    nc = build()
    res = run_bass_kernel_spmd(nc, in_maps, core_ids=list(range(8)))
    out = np.empty((2, 16384, 1024), np.float32)
    for core in range(8):
        b, qtr = divmod(core, 4)
        out[b, qtr * 4096:(qtr + 1) * 4096] = res.results[core]["out"]
    return out


if __name__ == "__main__":
    nc = build()
    print("instructions:", sum(1 for _ in nc.all_instructions()))


# version 25
# speedup vs baseline: 1.0611x; 1.0611x over previous
import numpy as np
import ml_dtypes
import concourse.bass as bass
import concourse.mybir as mybir
from concourse.bass_utils import run_bass_kernel_spmd
from concourse.bass import IndirectOffsetOnAxis

F32 = mybir.dt.float32
BF16 = mybir.dt.bfloat16
AF = mybir.ActivationFunctionType
ALU = mybir.AluOpType
AX = mybir.AxisListType

D = 1024
NCH = 8
NTB_ALL = 128
NTB_OWN = 32
FF = 4096
NE = 34
NQ = 4
QTB = 8
EPS = 1e-5
QK_SCALE = 0.25
CAP = 1536
TPE = CAP // 128

ENGS = ("pe", "act", "dve", "sy", "gp")


class Prog:
    def __init__(self):
        self.ops = {e: [] for e in ENGS}
        self.ctr = {}
        self.sem_names = set()

    def op(self, eng, fn, waits=(), inc=None, inc_n=1):
        w = [x for x in waits if x is not None]
        for (s, v) in w:
            self.sem_names.add(s)
        self.ops[eng].append((fn, w, inc, inc_n))
        if inc is not None:
            self.sem_names.add(inc)
            self.ctr[inc] = self.ctr.get(inc, 0) + inc_n
            return (inc, self.ctr[inc])
        return None

    def emit(self, block, sems):
        def mk(name):
            def run(engine):
                for fn, waits, inc, inc_n in self.ops[name]:
                    for (s, v) in waits:
                        engine.wait_ge(sems[s], v)
                    ins = fn(engine) if fn is not None else None
                    if inc is not None:
                        ins.then_inc(sems[inc], inc_n)
            return run
        block.tensor(mk("pe"))
        block.scalar(mk("act"))
        block.vector(mk("dve"))
        block.sync(mk("sy"))
        block.gpsimd(mk("gp"))


class Stream:

    def __init__(self, prog, name, qeng):
        self.p, self.name, self.qeng = prog, name, qeng
        self.n = 0

    def dma(self, out, in_, waits=()):
        w = list(waits)
        if self.n > 0:
            w.append((self.name, 16 * self.n))
        m = self.p.op(self.qeng, lambda e, o=out, i=in_: e.dma_start(out=o, in_=i),
                      waits=w, inc=self.name, inc_n=16)
        self.n += 1
        return m


def _r3(ap, p=128):
    return ap.rearrange("(c p) f -> p c f", p=p)


def build(dbg=False, upto='E', dsub=9):
    nc = bass.Bass("TRN2", target_bir_lowering=False, detect_race_conditions=False)

    def din(name, shape, dt=F32):
        return nc.dram_tensor(name, shape, dt, kind="ExternalInput").ap()

    x_in = din("x", [NTB_ALL * 128, D])
    wq_in = din("wq", [D, D]); wk_in = din("wk", [D, D]); wv_in = din("wv", [D, D])
    wo_in = din("wo", [D, D]); wd_in = din("wd", [D, D])
    ff1_in = din("ff1", [D, FF]); ff2_in = din("ff2", [FF, D])
    g1_in = din("g1", [128, 8]); b1_in = din("b1", [128, 8])
    g2_in = din("g2", [128, 8]); b2_in = din("b2", [128, 8])
    bff1_in = din("bff1", [128, 32])
    bo_rep_in = din("bo_rep", [128, D])
    bff2_rep_in = din("bff2_rep", [128, D])
    bd_rep_in = din("bd_rep", [128, D])
    esc_in = din("esc_rep", [128, 32])
    gwT_in = din("gwT", [D, 32])
    egu_in = din("egu", [NE, D, D])
    edn_in = din("edn", [NE, 512, D])
    identb_in = din("identb", [128, 128], BF16)
    identf_in = din("identf", [128, 128])
    lt128_in = din("lt128", [128, 128], BF16)
    lt32_in = din("lt32", [32, 32], BF16)
    eoffm1_in = din("eoffm1", [128, 32])
    I32 = mybir.dt.int32

    out_dram = nc.dram_tensor("out", [NTB_OWN * 128, D], F32, kind="ExternalOutput").ap()
    spk = "ExternalOutput" if dbg else "Internal"
    ybo_sp = nc.dram_tensor("ybo_sp", [NTB_OWN * 128, D], F32, kind=spk).ap()
    q_sp = nc.dram_tensor("q_sp", [NTB_OWN, 128, D], BF16, kind=spk).ap()
    xat_sp = nc.dram_tensor("xat_sp", [NTB_OWN * 128, D], F32, kind=spk).ap()
    h2_sp = nc.dram_tensor("h2_sp", [NCH, 128, NTB_OWN * 128], BF16, kind=spk).ap()
    x3_sp = nc.dram_tensor("x3_sp", [NTB_OWN * 128, D], F32, kind=spk).ap()
    o1_sp = nc.dram_tensor("o1_sp", [NTB_OWN * 128, D], F32, kind=spk).ap()
    xm_sp = nc.dram_tensor("xm_sp", [NCH, 128, NTB_OWN * 128], BF16, kind=spk).ap()
    xmt_sp = nc.dram_tensor("xmt_sp", [NTB_OWN * 128, D], BF16, kind=spk).ap()
    xg_dram = nc.dram_tensor("xg_dram", [32 * CAP + 1, D], BF16, kind="Internal").ap()
    egu_bf = nc.dram_tensor("egu_bf", [NE, D, D], BF16, kind="Internal").ap()
    edn_bf = nc.dram_tensor("edn_bf", [NE, 512, D], BF16, kind="Internal").ap()
    crow_dram = nc.dram_tensor("crow_dram", [1, NTB_OWN * 32], F32, kind="Internal").ap()
    brow_dram = nc.dram_tensor("brow_dram", [NTB_OWN, 32], F32, kind="Internal").ap()
    outp_dram = nc.dram_tensor("outp_dram", [32 * CAP + 1, D], BF16, kind=spk if dbg else "Internal").ap()
    if dbg:
        ctx_db = nc.dram_tensor("ctx_db", [128, 2048], F32, kind="ExternalOutput").ap()
        z_db = nc.dram_tensor("z_db", [128, 8], F32, kind="ExternalOutput").ap()
        cmb_db = nc.dram_tensor("cmb_db", [128, NTB_OWN * NE], F32, kind="ExternalOutput").ap()
        posi_db = nc.dram_tensor("posi_db", [128, 6 * NTB_OWN], I32, kind="ExternalOutput").ap()
        wgtj_db = nc.dram_tensor("wgtj_db", [128, 6 * NTB_OWN], F32, kind="ExternalOutput").ap()

    SB_OFS = 16384

    def sb(name, shape, dt, at):
        return nc.alloc_sbuf_tensor_at(name, shape, dt, offset=SB_OFS + at)

    identb = sb("identb", [128, 128], BF16, 0)
    identf = sb("identf", [128, 128], F32, 256)
    onesb = sb("onesb", [128, 8], BF16, 768)
    g1c = sb("g1c", [128, 8], F32, 800)
    b1c = sb("b1c", [128, 8], F32, 832)
    g2c = sb("g2c", [128, 8], F32, 864)
    b2c = sb("b2c", [128, 8], F32, 896)
    bff1c = sb("bff1c", [128, 32], F32, 928)
    gwTs = sb("gwTs", [128, 256], F32, 1056)
    esc = sb("esc", [128, 32], F32, 2080)
    stat = sb("stat", [128, 32], F32, 2208)
    qs8 = sb("qs8", [128, 8], F32, 2336)
    zr8 = sb("zr8", [128, 8], F32, 2368)
    cst4 = sb("cst4", [128, 8], F32, 2400)
    nc.const_aps.register(F32, 0.0, cst4[:, 0:1])
    nc.const_aps.register(F32, 1.0, cst4[:, 1:2])
    nc.const_aps.register(F32, 0.25, cst4[:, 2:3])
    nc.const_aps.register(F32, 1.0 / D, cst4[:, 3:4])
    nc.const_aps.register(F32, EPS, cst4[:, 4:5])
    nc.const_aps.register(F32, -0.5, cst4[:, 5:6])
    nc.const_aps.register(F32, EPS * EPS, cst4[:, 6:7])
    nc.const_aps.register(F32, 1.0 + EPS, cst4[:, 7:8])
    xt = [sb(f"xt{p}", [128, D], F32, 2432 + 4096 * p) for p in range(2)]
    yt = [sb(f"yt{p}", [128, D], F32, 10624 + 4096 * p) for p in range(2)]
    yb = [sb(f"yb{p}", [128, D], F32, 18816 + 4096 * p) for p in range(2)]
    qe = sb("qe", [128, D], F32, 27008)
    bfa = sb("bfa", [128, D], BF16, 31104)
    bfb = sb("bfb", [128, D], BF16, 33152)
    bfc = [sb(f"bfc{p}", [128, D], BF16, 35200 + 2048 * p) for p in range(2)]
    bfd = [sb(f"bfd{p}", [128, D], BF16, 39296 + 2048 * p) for p in range(2)]
    qsf = sb("qsf", [128, D], BF16, 43392)
    qfm = [sb(f"qfm{p}", [128, D], BF16, 45440 + 2048 * p) for p in range(2)]
    xm32 = sb("xm32", [128, D], F32, 49536)
    MID = 53632
    ctx_sb = sb("ctx_sb", [128, 2048], F32, MID)
    zsb = sb("zsb", [128, 8], F32, MID + 8192)
    ctxn = sb("ctxn", [128, 2048], BF16, MID + 8224)
    a_fm = sb("a_fm", [128, 8192], BF16, MID)
    combine = sb("combine", [128, NTB_OWN * NE], F32, MID)
    rsc1 = sb("rsc1", [128, NTB_OWN * NE], F32, MID + 4352)
    rsc2 = sb("rsc2", [128, NTB_OWN * NE], F32, MID + 8704)
    h2sl = [sb(f"h2sl{p}", [128, 2048], BF16, 10624 + 4096 * p) for p in range(2)]
    BASE = 70048
    SLOT = 65536
    assert 16384 + BASE + 2 * SLOT <= 229344
    wq = sb("wq", [128, NCH * D], BF16, BASE)
    wk = sb("wk", [128, NCH * D], BF16, BASE + 16384)
    wv = sb("wv", [128, NCH * D], BF16, BASE + 32768)
    ff1 = sb("ff1", [128, NCH * FF], BF16, BASE)
    wd = sb("wd", [128, NCH * D], BF16, BASE)
    bd_rep = sb("bd_rep", [128, D], F32, BASE + 16384)
    acc = sb("acc", [128, QTB * D], F32, BASE)
    xmq = [sb(f"xmq{p}", [128, NCH * QTB * 128], BF16, BASE + 32768 + 16384 * p) for p in range(2)]
    wo = sb("wo", [128, NCH * D], BF16, BASE + SLOT)
    bo_rep = sb("bo_rep", [128, D], F32, BASE + SLOT + 16384)
    bff2_rep = sb("bff2_rep", [128, D], F32, BASE + SLOT + 20480)
    ff2 = sb("ff2", [128, 32 * D], BF16, BASE + SLOT)
    egu = [sb(f"egu{p}", [128, NCH * D], BF16, BASE + SLOT + 16384 * p) for p in range(2)]
    edn = [sb(f"edn{p}", [128, 4 * D], BF16, BASE + SLOT + 32768 + 8192 * p) for p in range(2)]
    TOP = BASE + 2 * SLOT
    qeB = sb("qeB", [128, D], F32, TOP)
    qeD = sb("qeD", [128, D], F32, TOP + 4096)
    obf = [sb(f"obf{i}", [128, D], BF16, TOP + 2048 * i) for i in range(4)]
    qeD2 = sb("qeD2", [128, D], F32, BASE + SLOT + 49152)
    ybb = [sb(f"ybb{i}", [128, D], BF16, 18816 + 2048 * i) for i in range(4)]
    qfm4e = [sb(f"qfm{i}e", [128, D], BF16, 49536 + 2048 * i) for i in range(2)]
    qeC = sb("qeC", [128, D], F32, 49536)
    qex = [sb(f"qex{p}", [128, D], F32, MID + 4096 * p) for p in range(2)]
    A1 = BASE + SLOT + 24576
    xt2 = [sb(f"xt{p+2}", [128, D], F32, A1 + 4096 * p) for p in range(2)]
    yt2 = [sb(f"yt{p+2}", [128, D], F32, A1 + 8192 + 4096 * p) for p in range(2)]
    yb2 = [sb(f"yb{p+2}", [128, D], F32, A1 + 16384 + 4096 * p) for p in range(2)]
    qeE = sb("qeE", [128, D], F32, A1 + 24576)
    qeF = sb("qeF", [128, D], F32, A1 + 28672)
    qeG = sb("qeG", [128, D], F32, A1 + 32768)
    qeH = sb("qeH", [128, D], F32, A1 + 36864)
    stat2 = sb("stat2", [128, 32], F32, MID + 12320)
    posi = sb("posi", [128, 6 * 32], I32, MID + 13056)
    wgtj = sb("wgtj", [128, 6 * 32], F32, MID + 13824)
    lt32 = sb("lt32", [32, 32], BF16, MID + 14592)
    eoffm1 = sb("eoffm1", [128, 32], F32, MID + 14656)
    lt128 = sb("lt128", [128, 128], BF16, MID + 14784)
    onesf = sb("onesf", [1, 128], F32, MID + 15040)
    mks = sb("mks", [128, 6 * NTB_OWN * 32], BF16, BASE + SLOT + 49152)
    ctb = sb("ctb", [32, 32], BF16, 10624 + 4096)
    bsb = sb("bsb", [32, 32], F32, 18816)
    gbt = [sb(f"gbt{i}", [128, D], BF16, BASE + SLOT + 2048 * i) for i in range(16)]
    bfa1 = sb("bfa1", [128, D], BF16, BASE + 49152)
    bfb1 = sb("bfb1", [128, D], BF16, BASE + 51200)
    bfa2_ = sb("bfa2_", [128, D], BF16, BASE + 53248)
    bfa3_ = sb("bfa3_", [128, D], BF16, BASE + 55296)
    bfb2_ = sb("bfb2_", [128, D], BF16, BASE + 57344)
    bfb3_ = sb("bfb3_", [128, D], BF16, BASE + 59392)
    assert 16384 + TOP + 8192 <= 229344
    assert A1 + 40960 <= BASE + 2 * SLOT

    PS = nc.alloc_psum_tensor("PS", [128, 4096], F32)

    def bank(b, n=1):
        return PS[:, b * 512:(b + n) * 512]

    PSb = PS[:].bitcast(BF16)

    def bankb(b):
        return PSb[:, b * 1024:(b + 1) * 1024]

    p = Prog()
    SPE, SACT, SDVE = "spe", "sact", "sdve"

    st_x = [Stream(p, f"sx{i}", "sy") for i in range(4)]
    st_y = [Stream(p, f"sy{i}", "sy") for i in range(2)]
    st_q = [Stream(p, f"sq{i}", "sy") for i in range(2)]
    st_a = [Stream(p, f"sa{i}", "gp") for i in range(2)]
    st_h = [Stream(p, f"sh{i}", "gp") for i in range(2)]
    st_3 = [Stream(p, f"s3{i}", "sy") for i in range(2)]
    st_o = [Stream(p, f"so{i}", "sy") for i in range(2)]
    st_m = [Stream(p, f"sm{i}", "gp") for i in range(2)]
    st_w = [Stream(p, f"sw{i}", "sy") for i in range(2)]
    st_t = [Stream(p, f"st{i}", "gp") for i in range(2)]
    st_sc = Stream(p, "ssc", "gp")
    st_s4 = [Stream(p, f"sc{i}s", "gp") for i in range(4)]
    st_gt = [Stream(p, f"sg{i}x", "gp") for i in range(8)]
    st_c = Stream(p, "scst", "sy")
    st_g = Stream(p, "sgw", "gp")
    st_e = [Stream(p, f"se{i}", "gp") for i in range(4)]
    st_n = [Stream(p, f"sn{i}", "gp") for i in range(2)]
    st_pc = [Stream(p, f"pc{i}", "gp") for i in range(4)]

    st_c.dma(identb[:], identb_in)
    st_c.dma(g1c[:], g1_in)
    st_c.dma(b1c[:], b1_in)
    st_c.dma(g2c[:], g2_in)
    st_c.dma(b2c[:], b2_in)
    st_c.dma(bff1c[:], bff1_in)
    m_bo = st_c.dma(bo_rep[:], bo_rep_in)
    st_c.dma(identf[:], identf_in)
    C_IDENTB = ("scst", 16)
    C_LN1 = ("scst", 16 * 3)
    C_LN2 = ("scst", 16 * 5)
    C_BFF1 = ("scst", 16 * 6)
    C_IDENTF = ("scst", 16 * 8)
    C_ESC = ("scst", 16 * 9)

    p.op("dve", lambda e: e.memset(onesb[:], 1.0), inc=SDVE)
    p.op("dve", lambda e: e.memset(cst4[:, 0:1], 0.0), inc=SDVE)
    p.op("dve", lambda e: e.memset(cst4[:, 1:2], 1.0), inc=SDVE)
    p.op("dve", lambda e: e.memset(cst4[:, 2:3], 0.25), inc=SDVE)
    p.op("dve", lambda e: e.memset(cst4[:, 3:4], 1.0 / D), inc=SDVE)
    p.op("dve", lambda e: e.memset(cst4[:, 4:5], EPS), inc=SDVE)
    p.op("dve", lambda e: e.memset(cst4[:, 5:6], -0.5), inc=SDVE)
    p.op("dve", lambda e: e.memset(cst4[:, 6:7], EPS * EPS), inc=SDVE)
    p.op("dve", lambda e: e.memset(cst4[:, 7:8], 1.0 + EPS), inc=SDVE)

    st_g2 = [Stream(p, f"g2{i}", "gp") for i in range(3)]
    m_wq = st_g.dma(wq[:].rearrange("p (c f) -> p c f", c=NCH), _r3(wq_in))
    m_wk = st_g2[0].dma(wk[:].rearrange("p (c f) -> p c f", c=NCH), _r3(wk_in))
    m_wv = st_g2[1].dma(wv[:].rearrange("p (c f) -> p c f", c=NCH), _r3(wv_in))
    m_wo = st_g2[2].dma(wo[:].rearrange("p (c f) -> p c f", c=NCH), _r3(wo_in))

    ln_prev = {}
    qe_rd = {}

    def ln_stats(src, base, m_ready, scratch, dve_extra=(), act_extra=(), stile=None):
        st_ = stat if stile is None else stile
        key = (st_.name, base)
        s1 = st_[:, base:base + 1]; m2 = st_[:, base + 1:base + 2]
        v1 = st_[:, base + 2:base + 3]; sd = st_[:, base + 3:base + 4]
        rstd = st_[:, base + 4:base + 5]; bv = st_[:, base + 5:base + 6]
        lnv = st_[:, base + 6:base + 7]
        ma1 = p.op("dve", lambda e, s=src, q=scratch:
                   e.tensor_tensor(q[:], s, s, ALU.mult),
                   waits=list(act_extra) + [m_ready, qe_rd.get(scratch.name)], inc=SDVE)
        md1 = p.op("dve", lambda e, s=src, o=s1: e.reduce_sum(o, s, axis=AX.X),
                   waits=[m_ready, ln_prev.get(key)] + list(dve_extra), inc=SDVE)
        mdr = p.op("dve", lambda e, o=v1, q=scratch: e.reduce_sum(o, q[:], axis=AX.X),
                   waits=[ma1], inc=SDVE)
        qe_rd[scratch.name] = mdr
        md2 = p.op("dve", lambda e, i=s1, o=m2:
                   e.scalar_tensor_tensor(o, i, 1.0 / (D * D), i, ALU.mult, ALU.mult),
                   waits=[md1], inc=SDVE)
        md3 = p.op("dve", lambda e, i=v1, o=sd, m=m2:
                   e.scalar_tensor_tensor(o, i, 1.0 / D, m, ALU.mult, ALU.subtract),
                   waits=[mdr, md2], inc=SDVE)
        ma2 = p.op("act", lambda e, i=sd, o=lnv:
                   e.activation(o, i, AF.Ln, bias=EPS, scale=1.0), waits=[md3], inc=SACT)
        ma3 = p.op("act", lambda e, i=lnv, o=rstd:
                   e.activation(o, i, AF.Exp, bias=0.0, scale=-0.5), inc=SACT)
        md4 = p.op("dve", lambda e, i=s1, r=rstd, o=bv:
                   e.scalar_tensor_tensor(o, i, -1.0 / D, r, ALU.mult, ALU.mult),
                   waits=[ma3], inc=SDVE)
        ln_prev[key] = md1
        return rstd, bv, md1, md4, s1, sd, md3

    m_y_done = [None] * 4
    m_yt_dve = [None] * 4
    m_hn = {}
    m_tr = {}
    m_hev = {}
    m_k = {}
    m_v = {}
    m_q = {}
    m_ek = {}
    m_vv = {}
    m_qe = {}
    m_ctxm = {}
    m_mq3 = {}
    m_kq_free = None
    ost4 = [qe, qeB, qeE, qeF]
    ist4 = [qeC, qeD, qeG, qeH]
    bfa2 = [bfa, bfa1, bfa2_, bfa3_]
    bfb2 = [bfb, bfb1, bfb2_, bfb3_]
    qsf2 = [qsf, qfm[0]]
    xt4 = xt + xt2
    yt4 = yt + yt2
    yb4 = yb + yb2
    stile4 = [stat, stat, stat2, stat2]
    nq_own = [0]

    def stats_block(i):
        par4 = i % 4
        par2 = i % 2
        own = i < NTB_OWN
        st_x[par4].dma(xt4[par4][:], x_in[i * 128:(i + 1) * 128, :],
                       waits=[m_y_done[par4]])
        rstd1, bv1, md1o, md4o, s1o, vo, md3o = ln_stats(xt4[par4][:], (par4 % 2) * 8,
                                       (f"sx{par4}", 16 * st_x[par4].n),
                                       ost4[par4], stile=stile4[par4])
        if own:
            my = p.op("act", lambda e, s=xt4[par4][:], o=yt4[par4][:], r=rstd1, b=bv1:
                      e.activation(o, s, AF.Identity, bias=b, scale=r),
                      waits=[md4o, m_yt_dve[par4]], inc=SACT)
            w_b = [my, m_bo]
            if i >= 4:
                w_b.append((f"sy{par2}", 16 * st_y[par2].n))
            mb = p.op("dve", lambda e, y=yt4[par4][:], o=yb4[par4][:]:
                      e.tensor_tensor(o, y, bo_rep[:], ALU.add), waits=w_b, inc=SDVE)
            st_y[par2].dma(ybo_sp[i * 128:(i + 1) * 128, :], yb4[par4][:], waits=[mb])
            m_yt_dve[par4] = mb
        sB = 16 + (par4 % 2) * 8
        r2t = stile4[par4][:, sB:sB + 1]
        bvh = stile4[par4][:, sB + 1:sB + 2]
        mh2 = p.op("act", lambda e, i_=vo, o=r2t:
                   e.activation(o, i_, AF.Ln, bias=EPS * EPS, scale=1.0 + EPS),
                   waits=[md3o], inc=SACT)
        mh3 = p.op("act", lambda e, i_=r2t, o=r2t:
                   e.activation(o, i_, AF.Exp, bias=0.0, scale=-0.5), inc=SACT)
        mh4 = p.op("dve", lambda e, i_=s1o, r=r2t, o=bvh:
                   e.scalar_tensor_tensor(o, i_, -1.0 / D, r, ALU.mult, ALU.mult),
                   waits=[mh3, m_hn.get(i - 4)], inc=SDVE)
        m_hn[i] = p.op("act", lambda e, s=xt4[par4][:], r=r2t, b=bvh, o=bfa2[i % 4][:]:
                       e.activation(o, s, AF.Identity, bias=b, scale=r),
                       waits=[mh4, m_tr.get(i - 4)], inc=SACT)
        m_y_done[par4] = m_hn[i]

    stats_block(0)
    stats_block(1)
    stats_block(2)
    for c in range(NTB_ALL + 2):
        if c < NTB_ALL:
            def tr_h(e, pp=c % 4):
                for ch in range(NCH):
                    mm = e.transpose(bankb(7)[:, ch * 128:(ch + 1) * 128],
                                     bfa2[pp][:, ch * 128:(ch + 1) * 128], identb[:])
                return mm
            m_tr[c] = p.op("pe", tr_h, waits=[m_hn[c], m_hev.get(c - 1), C_IDENTB], inc=SPE)
        if 0 <= c - 1 < NTB_ALL:
            j = c - 1

            def proj(e, wgt, pp):
                for ch in range(NCH):
                    lhs = bfb2[pp][:, ch * 128:(ch + 1) * 128]
                    e.matmul(bank(5), lhs, wgt[:, ch * D:ch * D + 512],
                             start=(ch == 0), stop=(ch == NCH - 1))
                    mm = e.matmul(bank(6), lhs, wgt[:, ch * D + 512:(ch + 1) * D],
                                  start=(ch == 0), stop=(ch == NCH - 1))
                return mm
            m_k[j] = p.op("pe", lambda e, j=j: proj(e, wk, j % 4),
                          waits=[m_hev[j], m_kq_free, m_wk], inc=SPE)
            m_ek[j] = p.op("act", lambda e, o=bfc[j % 2][:]:
                           e.activation(o, bank(5, 2), AF.Exp, scale=QK_SCALE),
                           waits=[m_k[j], m_ctxm.get(j - 2)], inc=SACT)
        if 0 <= c - 2 < NTB_ALL:
            j = c - 2

            def ctxmm(e, first=(j == 0), last=(j == NTB_ALL - 1), pp=j % 2):
                for k in range(8):
                    h = k // 2
                    e.matmul(PS[:, k * 256:(k + 1) * 256], bfc[pp][:, k * 128:(k + 1) * 128],
                             bfd[pp][:, h * 256:(h + 1) * 256],
                             start=(first and k % 2 == 0), stop=(last and k % 2 == 1),
                             skip_group_check=True)
                for k in range(8):
                    mm = e.matmul(PS[:, 2048 + k:2049 + k], bfc[pp][:, k * 128:(k + 1) * 128],
                                  onesb[:, 0:1], start=(first and k == 0),
                                  stop=(last and k == 7), skip_group_check=True)
                return mm
            m_ctxm[j] = p.op("pe", ctxmm, waits=[m_vv[j]], inc=SPE)
        if c + 3 < NTB_ALL:
            stats_block(c + 3)
        if 0 <= c - 1 < NTB_ALL:
            j = c - 1
            m_v[j] = p.op("pe", lambda e, j=j: proj(e, wv, j % 4),
                          waits=[m_ek[j], m_wv], inc=SPE)
            m_vv[j] = p.op("act", lambda e, o=bfd[j % 2][:]:
                           e.activation(o, bank(5, 2), AF.Identity), waits=[m_v[j]], inc=SACT)
            m_kq_free = m_vv[j]
            if c < NTB_ALL:
                pass
        if c < NTB_ALL:
            def hev(e, pp=c % 4):
                for ch in range(NCH):
                    ins = e.scalar_tensor_tensor(
                        bfb2[pp][:, ch * 128:(ch + 1) * 128],
                        bankb(7)[:, ch * 128:(ch + 1) * 128],
                        g1c[:, ch:ch + 1], b1c[:, ch:ch + 1].broadcast_to([128, 128]),
                        ALU.mult, ALU.add)
                return ins
            m_hev[c] = p.op("dve", hev,
                            waits=[m_tr[c], C_LN1, m_q.get(c - 4) or m_v.get(c - 4)],
                            inc=SDVE)
        if 0 <= c - 1 < min(NTB_OWN, NTB_ALL):
            j = c - 1
            qx = qex[j % 2]
            m_q[j] = p.op("pe", lambda e, j=j: proj(e, wq, j % 4),
                          waits=[m_vv[j], m_wq], inc=SPE)
            m_qe[j] = p.op("act", lambda e, o=qx[:]:
                           e.activation(o, bank(5, 2), AF.Exp, scale=QK_SCALE),
                           waits=[m_q[j], qe_rd.get(qx.name)], inc=SACT)
            m_kq_free = m_qe[j]
            mq1 = p.op("dve", lambda e, qx=qx: e.reduce_sum(
                qs8[:, (j % 2) * 4:(j % 2) * 4 + 4],
                qx[:].rearrange("p (h d) -> p h d", h=4), axis=AX.X),
                waits=[m_qe[j]], inc=SDVE)
            mq2 = p.op("dve", lambda e: e.reciprocal(zr8[:, (j % 2) * 4:(j % 2) * 4 + 4],
                                                     qs8[:, (j % 2) * 4:(j % 2) * 4 + 4]),
                       waits=[mq1], inc=SDVE)
            w_q3 = [mq2]
            if j >= 2:
                w_q3.append((f"sq{j % 2}", 16 * st_q[j % 2].n))
            mq3 = p.op("dve", lambda e, qx=qx, o=qsf2[j % 2]: e.tensor_tensor(
                o[:].rearrange("p (h d) -> p h d", h=4),
                qx[:].rearrange("p (h d) -> p h d", h=4),
                zr8[:, (j % 2) * 4:(j % 2) * 4 + 4].unsqueeze(2).broadcast_to([128, 4, 256]),
                ALU.mult), waits=w_q3, inc=SDVE)
            m_mq3[j] = mq3
            qe_rd[qx.name] = mq1
            st_q[j % 2].dma(q_sp[j], qsf2[j % 2][:], waits=[mq3])

    st_c.dma(esc[:], esc_in)
    m_gw = st_c.dma(gwTs[:].rearrange("p (c e) -> p c e", c=8), _r3(gwT_in))
    m_cst = st_c.dma(bff2_rep[:], bff2_rep_in)
    m_pcg = {}
    m_pcd = {}
    order = [32, 33] + list(range(32))
    for i, e_ in enumerate(order):
        w_pc = [m_wo] if i < 2 else [m_wo, m_hn[min(4 * (i - 2) + 8, 124)]]
        m_pcg[e_] = st_pc[i % 2].dma(egu_bf[e_:e_ + 1], egu_in[e_:e_ + 1], waits=w_pc)
        m_pcd[e_] = st_pc[2 + i % 2].dma(edn_bf[e_:e_ + 1], edn_in[e_:e_ + 1], waits=w_pc)

    fA_pe = m_ctxm[NTB_ALL - 1]
    fA_act = (SACT, p.ctr[SACT])
    fA_dve = p.op("dve", lambda e: e.engine_nop(), inc=SDVE)

    if upto >= 'B':
        m_ff1 = st_g.dma(ff1[:].rearrange("p (c f) -> p c f", c=NCH), _r3(ff1_in), waits=[fA_pe])
    def finish():
        waits = []
        for st in (st_x + st_y + st_q + st_a + st_h + st_3 + st_o + st_m + st_w +
                   [st_c, st_g] + st_e + st_n):
            if st.n:
                waits.append((st.name, 16 * st.n))
        waits += [(SPE, p.ctr.get(SPE, 0)), (SACT, p.ctr.get(SACT, 0)), (SDVE, p.ctr.get(SDVE, 0))]
        p.op("sy", None, waits=waits)
        import contextlib
        stack = contextlib.ExitStack()
        sems = {name: stack.enter_context(nc.semaphore(name)) for name in sorted(p.sem_names)}
        with nc.Block() as block:
            p.emit(block, sems)
        return nc

    if upto == 'A':
        return finish()


    me1 = p.op("dve", lambda e: e.tensor_copy(ctx_sb[:], PS[:, 0:2048]), waits=[fA_pe], inc=SDVE)
    me2 = p.op("dve", lambda e: e.tensor_copy(zsb[:], PS[:, 2048:2056]), inc=SDVE)
    me3 = p.op("dve", lambda e: e.reciprocal(zr8[:], zsb[:]), waits=[me2], inc=SDVE)
    me4 = p.op("dve", lambda e, _w=None: e.tensor_tensor(
        ctxn[:].rearrange("p (k e) -> p k e", k=8),
        ctx_sb[:].rearrange("p (k e) -> p k e", k=8),
        zr8[:].unsqueeze(2).broadcast_to([128, 8, 256]), ALU.mult),
        waits=[me3, me1], inc=SDVE)
    if dbg:
        st_c.dma(ctx_db, ctx_sb[:], waits=[me4])
        st_c.dma(z_db, zsb[:], waits=[me4])

    m_attn_ev = None
    m_attn_pe = [None, None]
    m_wo_dve = None
    m_x2n = [None, None]
    m_ptr_free_b = [fA_dve]
    m_tr2 = {}
    m_yb_rd = [None, None]
    m_xat_dve = [None, None]
    m_qtr = [None, None]
    m_qev = [None, None]
    bfaB = [bfa, bfb]
    st_yr = [Stream(p, f"yr{i}", "gp") for i in range(2)]
    m_ry = {}
    for j0 in range(2):
        m_ry[j0] = st_yr[j0].dma(yb[j0][:], ybo_sp[j0 * 128:(j0 + 1) * 128, :],
                                 waits=[(f"sy{i}", 16 * st_y[i].n) for i in range(2)])

    def b_tail(jt):
        pt = jt % 2

        def tr2(e, pt=pt):
            for c in range(NCH):
                mm = e.transpose(bankb(7)[:, c * 128:(c + 1) * 128],
                                 bfaB[pt][:, c * 128:(c + 1) * 128], identb[:])
            return mm
        mt2 = p.op("pe", tr2, waits=[m_x2n[pt], m_ptr_free_b[0]], inc=SPE)
        m_tr2[jt] = mt2

        def h2ev(e, pp=pt):
            for c in range(NCH):
                ins = e.scalar_tensor_tensor(
                    bfc[pp][:, c * 128:(c + 1) * 128], bankb(7)[:, c * 128:(c + 1) * 128],
                    g2c[:, c:c + 1], b2c[:, c:c + 1].broadcast_to([128, 128]),
                    ALU.mult, ALU.add)
            return ins
        w_h2 = [mt2, C_LN2]
        if jt >= 2:
            w_h2.append((f"sh{pt}", 16 * st_h[pt].n))
        mh2 = p.op("dve", h2ev, waits=w_h2, inc=SDVE)
        m_ptr_free_b[0] = mh2
        st_h[pt].dma(h2_sp.rearrange("c d t -> d c t")[:, :, jt * 128:(jt + 1) * 128],
                     bfc[pt][:].rearrange("p (c t) -> p c t", c=NCH), waits=[mh2])

    for j in range(NTB_OWN):
        par = j % 2
        mrq = st_q[par].dma(qfm[par][:], q_sp[j], waits=[m_qtr[par]])
        mry = m_ry[j]

        def tr_qb(e, pp=par):
            for c in range(NCH):
                mm = e.transpose(bankb(4)[:, c * 128:(c + 1) * 128],
                                 qfm[pp][:, c * 128:(c + 1) * 128], identb[:])
            return mm
        w_tq = [mrq, m_qev[par]]
        if j < 2:
            w_tq += [fA_act, fA_dve, me2]
        mtq = p.op("pe", tr_qb, waits=w_tq, inc=SPE)
        m_qtr[par] = mtq
        if j >= 1:
            b_tail(j - 1)
        mqv = p.op("act", lambda e, pp=par: e.activation(bfd[pp][:], bankb(4), AF.Identity),
                   waits=[mtq, m_attn_pe[par]], inc=SACT)
        m_qev[par] = mqv

        def attnmm(e, pp=par):
            for c in range(8):
                h, j2 = c // 2, c % 2
                for j1 in range(2):
                    k = 2 * h + j1
                    mm = e.matmul(PS[:, c * 128:(c + 1) * 128],
                                  ctxn[:, (k * 256 + j2 * 128):(k * 256 + j2 * 128 + 128)],
                                  bfd[pp][:, k * 128:(k + 1) * 128],
                                  start=(j1 == 0 and c % 4 == 0), stop=(j1 == 1),
                                  skip_group_check=True)
            return mm
        ma_ = p.op("pe", attnmm, waits=[me4, mqv, m_attn_ev], inc=SPE)
        m_attn_pe[par] = ma_
        mae = p.op("act", lambda e: e.activation(qsf[:], PS[:, 0:1024], AF.Identity),
                   waits=[ma_], inc=SACT)
        m_attn_ev = mae

        def womm(e):
            for c in range(NCH):
                lhs = qsf[:, c * 128:(c + 1) * 128]
                e.matmul(bank(2), lhs, wo[:, c * D:c * D + 512],
                         start=(c == 0), stop=(c == NCH - 1))
                mm = e.matmul(bank(3), lhs, wo[:, c * D + 512:(c + 1) * D],
                              start=(c == 0), stop=(c == NCH - 1))
            return mm
        mw_ = p.op("pe", womm, waits=[mae, m_wo, m_wo_dve], inc=SPE)
        w_x = [mw_, mry, m_xat_dve[par]]
        mxa = p.op("dve", lambda e, o=yt[par][:], b=yb[par][:]:
                   e.tensor_tensor(o, b, bank(2, 2), ALU.add), waits=w_x, inc=SDVE)
        m_wo_dve = mxa
        if j + 2 < NTB_OWN:
            m_ry[j + 2] = st_yr[par].dma(yb[par][:], ybo_sp[(j + 2) * 128:(j + 3) * 128, :],
                                         waits=[mxa])
        w_x2 = [mxa, m_cst]
        if j >= 2:
            w_x2.append((f"sa{par}", 16 * st_a[par].n))
        mxb = p.op("dve", lambda e, o=xt[par][:], i=yt[par][:]:
                   e.tensor_tensor(o, i, bff2_rep[:], ALU.add), waits=w_x2, inc=SDVE)
        st_a[par].dma(xat_sp[j * 128:(j + 1) * 128, :], xt[par][:], waits=[mxb])
        rstd2, bv2, md1b, md4b, _sB, _vB, _mB = ln_stats(yt[par][:], par * 8, mxa,
                                                         [qe, qeB][par])
        m_xat_dve[par] = md1b
        mx2 = p.op("act", lambda e, s=yt[par][:], r=rstd2, b=bv2, o=bfaB[par][:]:
                   e.activation(o, s, AF.Identity, bias=b, scale=r),
                   waits=[md4b, m_tr2.get(j - 2, fA_pe)], inc=SACT)
        m_x2n[par] = mx2
    b_tail(NTB_OWN - 1)

    fB_pe = (SPE, p.ctr[SPE])
    fB_act = (SACT, p.ctr[SACT])
    fB_dve = (SDVE, p.ctr[SDVE])
    nh = [st_h[0].n, st_h[1].n]
    if upto == 'B':
        return finish()
    m_ff2 = st_g.dma(ff2[:].rearrange("p (c f) -> p c f", c=32), _r3(ff2_in), waits=[fB_pe])

    m_gel = [None, None]
    m_x3dve = [None, None]
    m_ff2_dve = None
    m_pe_after_prev_slice = None
    m_f1_last = [None, None]
    for s in range(NTB_OWN // 2):
        spar = s % 2
        w_rh = [(f"sh{1 - spar}", 16 * nh[1 - spar])]
        if s < 2:
            w_rh += [fB_dve, fB_act]
        else:
            w_rh.append(m_f1_last[spar])
        mrh = st_h[spar].dma(h2sl[spar][:].rearrange("p (c t) -> p c t", c=NCH),
                             h2_sp[:, :, s * 256:(s + 1) * 256].rearrange("c d t -> d c t"),
                             waits=w_rh)
        for fb in range(32):
            bpar = fb % 2

            def f1mm(e, fb=fb, bpar=bpar, spar=spar):
                for c in range(NCH):
                    mm = e.matmul(PS[:, bpar * 512:bpar * 512 + 256],
                                  ff1[:, c * FF + fb * 128:c * FF + fb * 128 + 128],
                                  h2sl[spar][:, c * 256:(c + 1) * 256],
                                  start=(c == 0), stop=(c == NCH - 1))
                return mm
            w_f1 = [mrh, m_ff1, m_gel[bpar]]
            if s == 0 and fb < 2:
                w_f1 += [fB_act, fB_dve]
            mf1 = p.op("pe", f1mm, waits=w_f1, inc=SPE)
            if fb == 31:
                m_f1_last[spar] = mf1
            m_gel[bpar] = p.op("act", lambda e, fb=fb, bpar=bpar:
                               e.activation(a_fm[:, fb * 256:(fb + 1) * 256],
                                            PS[:, bpar * 512:bpar * 512 + 256],
                                            AF.Gelu_apprx_tanh, bias=bff1c[:, fb:fb + 1]),
                               waits=[mf1, C_BFF1], inc=SACT)
        m_gelu_all = m_gel[1]
        for tt_ in range(2):
            j = s * 2 + tt_
            par = j % 2
            mra = st_a[par].dma(xt[par][:], xat_sp[j * 128:(j + 1) * 128, :],
                                waits=[m_x3dve[par]])

            def f2mm(e, tt=tt_):
                for fc in range(32):
                    lhs = a_fm[:, fc * 256 + tt * 128:fc * 256 + tt * 128 + 128]
                    e.matmul(bank(2), lhs, ff2[:, fc * D:fc * D + 512],
                             start=(fc == 0), stop=(fc == 31))
                    mm = e.matmul(bank(3), lhs, ff2[:, fc * D + 512:(fc + 1) * D],
                                  start=(fc == 0), stop=(fc == 31))
                return mm
            w_f2 = [m_gelu_all, m_ff2, m_ff2_dve]
            if s == 0 and tt_ == 0:
                w_f2 += [fB_dve]
            mf2 = p.op("pe", f2mm, waits=w_f2, inc=SPE)
            m_pe_after_prev_slice = mf2
            mx3 = p.op("dve", lambda e, o=yb[par][:], i=xt[par][:]:
                       e.tensor_tensor(o, i, bank(2, 2), ALU.add), waits=[mf2, mra], inc=SDVE)
            m_ff2_dve = mx3
            m_x3dve[par] = mx3
            st_3[par].dma(x3_sp[j * 128:(j + 1) * 128, :], yb[par][:], waits=[mx3])

    fC_pe = (SPE, p.ctr[SPE])
    fC_act = (SACT, p.ctr[SACT])
    fC_dve = (SDVE, p.ctr[SDVE])
    if upto == 'C':
        return finish()
    m_xmsp = {}
    m_wd = st_g.dma(wd[:].rearrange("p (c f) -> p c f", c=NCH), _r3(wd_in), waits=[fC_pe])
    m_bd = st_c.dma(bd_rep[:], bd_rep_in, waits=[fC_pe])

    m_x3T_ev = None
    m_wd_dve = None
    m_xmT_ev = [None]
    m_xmT_act = [None]
    m_o1_dve = [None, None]
    m_trm_prev = None
    m_xm_act = [None, None]
    m_ybd = [None, None]
    m_o1sp = {}
    m_xmtsp = {}
    qeDp = [qeD, qeD2]
    st_og = [Stream(p, f"og{i}", "gp") for i in range(2)]

    def d_tail(jt):
        pt = jt % 2

        def trm(e, pt=pt):
            for c in range(NCH):
                mm = e.transpose(PS[:, 2048 + c * 128:2048 + (c + 1) * 128],
                                 qeDp[pt][:, c * 128:(c + 1) * 128], identf[:])
            return mm
        mtm = p.op("pe", trm, waits=[m_xm_act[pt], m_xmT_ev[0], m_xmT_act[0]], inc=SPE)
        qe_rd[qeDp[pt].name] = mtm
        w_mb = [mtm]
        if jt >= 2:
            w_mb.append((f"sm{pt}", 16 * st_m[pt].n))
        mmb = p.op("act", lambda e, pp=pt: e.activation(bfc[pp][:], PS[:, 2048:3072],
                                                        AF.Identity), waits=w_mb, inc=SACT)
        m_xmT_act[0] = mmb
        mm32 = p.op("dve", lambda e: e.tensor_copy(xm32[:], PS[:, 2048:3072]),
                    waits=[mtm, mmb, m_gate_ev[0]], inc=SDVE)
        m_xmT_ev[0] = mm32
        m_xmsp[jt] = st_m[pt].dma(
            xm_sp.rearrange("c d t -> d c t")[:, :, jt * 128:(jt + 1) * 128],
            bfc[pt][:].rearrange("p (c t) -> p c t", c=NCH), waits=[mmb])
        w_xt = [m_xm_act[pt]]
        if jt >= 2:
            w_xt.append((f"st{pt}", 16 * st_t[pt].n))
        mxtb = p.op("act", lambda e, pp=pt: e.activation(bfd[pp][:], qeDp[pt][:], AF.Identity),
                    waits=w_xt, inc=SACT)
        m_xmtsp[jt] = st_t[pt].dma(xmt_sp[jt * 128:(jt + 1) * 128, :], bfd[pt][:], waits=[mxtb])

        def gmm(e):
            for c in range(NCH):
                mm = e.matmul(PS[:, 3072:3104], xm32[:, c * 128:(c + 1) * 128],
                              gwTs[:, c * 32:(c + 1) * 32], start=(c == 0), stop=(c == NCH - 1))
            return mm
        mg = p.op("pe", gmm, waits=[mm32, m_gw, m_gate_ev[0]], inc=SPE)
        m_gate_ev[0] = p.op("dve", lambda e, jt=jt: e.tensor_copy(rsc1[:, jt * 32:(jt + 1) * 32],
                                                                  PS[:, 3072:3104]),
                            waits=[mg], inc=SDVE)

    m_gate_ev = [None]
    for j in range(NTB_OWN):
        par = j % 2
        mr3 = st_3[par].dma(xt[par][:], x3_sp[j * 128:(j + 1) * 128, :],
                            waits=[m_xm_act[par], m_o1sp.get(j - 2)])
        mrx = st_x[par].dma(yt[par][:], x_in[j * 128:(j + 1) * 128, :],
                            waits=[fC_pe] if j < 2 else [m_ybd[par]])

        def tr3(e, pp=par):
            for c in range(NCH):
                mm = e.transpose(PS[:, c * 128:(c + 1) * 128],
                                 xt[pp][:, c * 128:(c + 1) * 128], identf[:])
            return mm
        w_t3 = [mr3, m_x3T_ev, C_IDENTF]
        if j == 0:
            w_t3 += [fC_act, fC_dve]
        mt3 = p.op("pe", tr3, waits=w_t3, inc=SPE)
        mev3 = p.op("act", lambda e: e.activation(bfa[:], PS[:, 0:1024], AF.Identity),
                    waits=[mt3], inc=SACT)
        m_x3T_ev = mev3

        def wdmm(e):
            for c in range(NCH):
                lhs = bfa[:, c * 128:(c + 1) * 128]
                e.matmul(bank(2), lhs, wd[:, c * D:c * D + 512],
                         start=(c == 0), stop=(c == NCH - 1))
                mm = e.matmul(bank(3), lhs, wd[:, c * D + 512:(c + 1) * D],
                              start=(c == 0), stop=(c == NCH - 1))
            return mm
        w_wd = [mev3, m_wd, m_wd_dve]
        if j == 0:
            w_wd += [fC_dve]
        if dsub < 1:
            m_x3T_ev = mev3
            m_o1_dve[par] = mr3
            continue
        mwd = p.op("pe", wdmm, waits=w_wd, inc=SPE)
        mxb = p.op("dve", lambda e, o=yb[par][:], i=yt[par][:]:
                   e.tensor_tensor(o, i, bd_rep[:], ALU.add), waits=[mrx, m_bd, mr3], inc=SDVE)
        m_ybd[par] = mxb
        mo1 = p.op("dve", lambda e, o=xt[par][:], i=yb[par][:]:
                   e.tensor_tensor(o, i, bank(2, 2), ALU.add), waits=[mwd, mxb], inc=SDVE)
        m_wd_dve = mo1
        m_o1_dve[par] = mo1
        m_o1sp[j] = st_og[par].dma(o1_sp[j * 128:(j + 1) * 128, :], xt[par][:], waits=[mo1])
        if j >= 1:
            d_tail(j - 1)
        rstd3, bv3, _, md4d, _sD, _vD, _mD = ln_stats(xt[par][:], par * 8, mo1, [qe, qeB][par])
        mxm = p.op("act", lambda e, s=xt[par][:], r=rstd3, b=bv3, o=qeDp[par][:]:
                   e.activation(o, s, AF.Identity, bias=b, scale=r),
                   waits=[md4d, qe_rd.get(qeDp[par].name)], inc=SACT)
        m_xm_act[par] = mxm
    d_tail(NTB_OWN - 1)

    if dsub < 4:
        return finish()
    fC_pe_pre = fC_pe
    fD_pe_pre = (SPE, p.ctr[SPE])
    fD_act_pre = (SACT, p.ctr[SACT])
    fD_dve_pre = (SDVE, p.ctr[SDVE])
    st_xq = [Stream(p, f"xq{i}", "sy") for i in range(2)]
    m_xmq_pre = {}
    for Q in range(min(2, NQ)):
        m_xmq_pre[Q] = st_xq[Q % 2].dma(
            xmq[Q % 2][:].rearrange("p (c t) -> p c t", c=NCH),
            xm_sp[:, :, Q * QTB * 128:(Q + 1) * QTB * 128].rearrange("c d t -> d c t"),
            waits=[fC_pe_pre, m_xmsp[Q * QTB + QTB - 2], m_xmsp[Q * QTB + QTB - 1]])
    NT = NTB_OWN
    L = NT * 32
    G = NT * 8
    sc3 = rsc2[:, 0:L].rearrange("p (t e) -> p t e", t=NT)
    sfc3 = combine[:, 0:L].rearrange("p (t e) -> p t e", t=NT)
    sfc4 = combine[:, 0:L].rearrange("p (t g s) -> p t g s", t=NT, g=8)
    gsc = rsc1[:, 0:G].rearrange("p (t g) -> p t g", t=NT)
    wkg = xm32[:, 0:G].rearrange("p (t g) -> p t g", t=NT)
    mskg = xm32[:, G:2 * G].rearrange("p (t g) -> p t g", t=NT)
    wk6 = qe[:, 0:L].rearrange("p (t e) -> p t e", t=NT)
    msk6 = xm32[:, 0:L].rearrange("p (t e) -> p t e", t=NT)

    def ncol(idx):
        return rsc1[:, idx * NT:(idx + 1) * NT]

    rt = [qe_rd.get(qe.name), qe_rd.get(qeB.name)]

    def rop(fn, extra=()):
        m = p.op("dve", fn, waits=rt + list(extra), inc=SDVE)
        rt.clear(); rt.append(m)
        return m

    mr0 = p.op("act", lambda e: e.activation(rsc2[:, 0:L], rsc1[:, 0:L], AF.Sigmoid),
               waits=[m_gate_ev[0]], inc=SACT)
    rop(lambda e: e.tensor_tensor(
        sfc3, sc3, esc[:].unsqueeze(1).broadcast_to([128, NT, 32]), ALU.add),
        extra=[mr0, C_ESC])
    rop(lambda e: e.reduce_max(gsc, sfc4, axis=AX.X))
    rop(lambda e: e.tensor_copy(xm32[:, 0:G], rsc1[:, 0:G]))
    for r in range(4):
        mcol = ncol(8 + r)
        rop(lambda e, o=mcol: e.reduce_max(o, wkg, axis=AX.X))
        rop(lambda e, m=mcol: e.tensor_tensor(
            mskg, wkg, m.rearrange("p (t o) -> p t o", t=NT).broadcast_to([128, NT, 8]),
            ALU.is_ge))
        rop(lambda e: e.scalar_tensor_tensor(
            xm32[:, 0:G], xm32[:, G:2 * G], -1e9, xm32[:, 0:G], ALU.mult, ALU.add))
    th4 = ncol(11)
    rop(lambda e: e.tensor_tensor(
        mskg, gsc, th4.rearrange("p (t o) -> p t o", t=NT).broadcast_to([128, NT, 8]),
        ALU.is_ge))
    gm4 = xm32[:, G:2 * G].rearrange("p (t g) -> p t g", t=NT).unsqueeze(3) \
        .broadcast_to([128, NT, 8, 4])
    rop(lambda e: e.tensor_tensor(sfc4, sfc4, gm4, ALU.mult))
    rop(lambda e: e.tensor_copy(qe[:, 0:L], combine[:, 0:L]))
    for r in range(6):
        mcol = ncol(12 + r)
        rop(lambda e, o=mcol: e.reduce_max(o, wk6, axis=AX.X))
        rop(lambda e, m=mcol: e.tensor_tensor(
            msk6, wk6, m.rearrange("p (t o) -> p t o", t=NT).broadcast_to([128, NT, 32]),
            ALU.is_ge))
        rop(lambda e, r=r: e.tensor_copy(mks[:, r * L:(r + 1) * L], xm32[:, 0:L]))
        rop(lambda e: e.scalar_tensor_tensor(
            qe[:, 0:L], xm32[:, 0:L], -1e9, qe[:, 0:L], ALU.mult, ALU.add))
    th6 = ncol(17)
    rop(lambda e: e.tensor_tensor(
        msk6, sfc3, th6.rearrange("p (t o) -> p t o", t=NT).broadcast_to([128, NT, 32]),
        ALU.is_ge))
    rop(lambda e: e.tensor_tensor(qe[:, 0:L], xm32[:, 0:L], rsc2[:, 0:L], ALU.mult))
    rop(lambda e: e.reduce_sum(ncol(18).rearrange("p (t o) -> p t o", t=NT), wk6, axis=AX.X))
    rop(lambda e: e.tensor_scalar(ncol(19), ncol(18), 1e-20, None, ALU.add))
    rop(lambda e: e.reciprocal(ncol(20), ncol(19)))
    rop(lambda e: e.tensor_scalar(ncol(21), ncol(20), 2.5, None, ALU.mult))
    cmb3 = combine[:].rearrange("p (t e) -> p t e", e=NE)
    rop(lambda e: e.tensor_tensor(
        cmb3[:, :, 0:32], wk6,
        ncol(21).rearrange("p (t o) -> p t o", t=NT).broadcast_to([128, NT, 32]),
        ALU.mult))
    mcmb = rop(lambda e: e.memset(cmb3[:, :, 32:34], 1.0))
    qe_rd[qe.name] = mcmb
    if dbg:
        st_c.dma(cmb_db, combine[:], waits=[mcmb])

    m_lt = st_c.dma(lt128[:], lt128_in, waits=[fC_pe_pre])
    m_lt2 = st_c.dma(lt32[:], lt32_in)
    m_eo = st_c.dma(eoffm1[:], eoffm1_in)
    m_selbf = rop(lambda e: e.tensor_copy(qsf[:, 0:L], xm32[:, 0:L]))

    def rankmm(e):
        for tb in range(NT):
            mm = e.matmul(PS[:, 3072 + tb * 32:3072 + (tb + 1) * 32], lt128[:],
                          qsf[:, tb * 32:(tb + 1) * 32],
                          start=(tb % 16 == 0), stop=(tb % 16 == 15 or tb == NT - 1),
                          skip_group_check=True)
        return mm
    mrk = p.op("pe", rankmm, waits=[m_selbf, m_lt, fD_act_pre, fD_dve_pre], inc=SPE)
    mrk_ev = p.op("dve", lambda e: e.tensor_copy(rsc2[:, 0:L], PS[:, 3072:3072 + NT * 32]),
                  waits=[mrk], inc=SDVE)
    m_cr = st_c.dma(crow_dram[0:1, :], rsc2[127:128, 0:L], waits=[mrk_ev])
    m_ct = st_sc.dma(ctb[0:NT, :], crow_dram.rearrange("o (t e) -> (o t) e", t=NT),
                     waits=[m_cr])
    mbm = p.op("pe", lambda e: e.matmul(PS[0:NT, 3072:3104], lt32[0:NT, 0:NT], ctb[0:NT, :],
                                        start=True, stop=True),
               waits=[m_ct, m_lt2, mrk_ev], inc=SPE)
    mbe = p.op("dve", lambda e: e.tensor_copy(bsb[0:NT, :], PS[0:NT, 3072:3104]),
               waits=[mbm], inc=SDVE)
    m_b1 = st_c.dma(brow_dram, bsb[0:NT, :], waits=[mbe])
    m_br = st_c.dma(yb[1][0:1, 0:L], brow_dram.rearrange("t e -> (t e)").unsqueeze(0),
                    waits=[m_b1])
    monef = p.op("dve", lambda e: e.memset(onesf[:], 1.0), inc=SDVE)

    def bbc(e):
        for of in range(0, L, 512):
            w = min(512, L - of)
            mm = e.matmul(PS[:, 3072 + of:3072 + of + w], onesf[0:1, :], yb[1][0:1, of:of + w],
                          start=True, stop=True)
        return mm
    mbb = p.op("pe", bbc, waits=[m_br, monef, mrk_ev], inc=SPE)
    mbbe = p.op("dve", lambda e: e.tensor_copy(rsc1[:, 0:L], PS[:, 3072:3072 + L]),
                waits=[mbb], inc=SDVE)
    rt2 = [mbbe, mrk_ev]

    def sop(fn, extra=()):
        rt2[0] = p.op("dve", fn, waits=rt2 + list(extra), inc=SDVE)
        del rt2[1:]
        return rt2[0]

    sop(lambda e: e.tensor_tensor(rsc2[:, 0:L], rsc2[:, 0:L], rsc1[:, 0:L], ALU.add))
    sop(lambda e: e.tensor_scalar(qe[:, 0:L], rsc2[:, 0:L], CAP + 0.5, None, ALU.is_gt))
    sop(lambda e: e.tensor_tensor(
        rsc2[:, 0:L].rearrange("p (t e) -> p t e", t=NT),
        rsc2[:, 0:L].rearrange("p (t e) -> p t e", t=NT),
        eoffm1[:].unsqueeze(1).broadcast_to([128, NT, 32]), ALU.add), extra=[m_eo])
    sop(lambda e: e.scalar_tensor_tensor(rsc2[:, 0:L], qe[:, 0:L], 1e9, rsc2[:, 0:L],
                                         ALU.mult, ALU.add))
    sop(lambda e: e.tensor_scalar(qeB[:, 0:L], xm32[:, 0:L], -1e9, 1e9, ALU.mult, ALU.add))
    sop(lambda e: e.tensor_tensor(rsc2[:, 0:L], rsc2[:, 0:L], qeB[:, 0:L], ALU.add))
    sop(lambda e: e.tensor_scalar(rsc2[:, 0:L], rsc2[:, 0:L], float(32 * CAP), None, ALU.min))
    for jj in range(6):
        mkj = mks[:, jj * L:(jj + 1) * L].rearrange("p (t e) -> p t e", t=NT)
        sop(lambda e, mkj=mkj: e.tensor_tensor(
            qe[:, 0:L].rearrange("p (t e) -> p t e", t=NT),
            rsc2[:, 0:L].rearrange("p (t e) -> p t e", t=NT), mkj, ALU.mult))
        sop(lambda e: e.reduce_sum(yt[1][0:128, 0:NT],
                                   qe[:, 0:L].rearrange("p (t e) -> p t e", t=NT), axis=AX.X))
        sop(lambda e, jj=jj: e.tensor_copy(posi[:, jj * NT:(jj + 1) * NT], yt[1][0:128, 0:NT]))
        sop(lambda e, mkj=mkj: e.tensor_tensor(
            qe[:, 0:L].rearrange("p (t e) -> p t e", t=NT),
            combine[:, 0:NT * NE].rearrange("p (t e) -> p t e", t=NT)[:, :, 0:32],
            mkj, ALU.mult))
        sop(lambda e, jj=jj: e.reduce_sum(wgtj[:, jj * NT:(jj + 1) * NT],
                                          qe[:, 0:L].rearrange("p (t e) -> p t e", t=NT),
                                          axis=AX.X))
    m_pos = rt2[0]
    if dbg:
        st_c.dma(posi_db, posi[:, 0:6 * NT], waits=[m_pos])
        st_c.dma(wgtj_db, wgtj[:, 0:6 * NT], waits=[m_pos])
    m_egu = {}
    m_edn = {}
    for ei, e_ in enumerate([32, 33]):
        m_egu[(0, ei)] = st_e[ei % 2].dma(
            egu[ei % 2][:].rearrange("p (c f) -> p c f", c=NCH), _r3(egu_bf[e_]),
            waits=[fC_pe_pre, m_pcg[e_]])
        m_edn[(0, ei)] = st_n[ei % 2].dma(
            edn[ei % 2][:].rearrange("p (c f) -> p c f", c=4), _r3(edn_bf[e_]),
            waits=[fC_pe_pre, m_pcd[e_]])
    mzr = p.op("dve", lambda e: e.memset(obf[0][0:1, :], 0.0), waits=[m_cr], inc=SDVE)
    st_c.dma(outp_dram[32 * CAP:32 * CAP + 1, :], obf[0][0:1, :], waits=[mzr])
    qfm4 = qfm + qfm4e
    msc = {}
    for tb in range(NT):
        par4 = tb % 4
        mrl = st_t[tb % 2].dma(qfm4[par4][:], xmt_sp[tb * 128:(tb + 1) * 128, :],
                               waits=msc.get(tb - 4, [None]))
        msc[tb] = []
        for jj in range(6):
            sq = st_s4[(tb * 6 + jj) % 4]
            msc[tb].append(p.op("gp", lambda e, par4=par4, jj=jj, tb=tb: e.indirect_dma_start(
                out=xg_dram, out_offset=IndirectOffsetOnAxis(
                    ap=posi[:, jj * NT + tb:jj * NT + tb + 1], axis=0),
                in_=qfm4[par4][:], in_offset=None),
                waits=[mrl, m_pos, (sq.name, 16 * sq.n)], inc=sq.name, inc_n=16))
            sq.n += 1
    f_scat4 = [(s.name, 16 * s.n) for s in st_s4]

    fD_pe = (SPE, p.ctr[SPE])
    fD_act = (SACT, p.ctr[SACT])
    nm = [st_m[0].n, st_m[1].n]
    if upto == 'D':
        return finish()

    DE = [32, 33]
    m_xmq = [None, None]
    m_act_done = {}
    m_tt = {}
    m_evac = {}
    m_acc = {}
    m_pe_gu = {}
    m_pe_tr = {}
    m_pe_dn = {}
    m_final = []
    m_fin_t = {}
    m_out_prev = {}
    SQ = len(DE) * QTB

    m_xmq_sub = {}

    def emit_xmq_sub(Q):
        qpar = Q % 2
        for h in range(QTB // 2):
            t0 = 2 * h
            m_xmq_sub[(qpar, h)] = st_xq[h % 2].dma(
                xmq[qpar][:].rearrange("p (c t) -> p c t", c=NCH, t=QTB * 128)
                [:, :, t0 * 128:(t0 + 2) * 128],
                xm_sp[:, :, (Q * QTB + t0) * 128:(Q * QTB + t0 + 2) * 128]
                .rearrange("c d t -> d c t"),
                waits=[m_xmsp[Q * QTB + t0], m_xmsp[Q * QTB + t0 + 1],
                       m_pe_gu[(Q - 2, len(DE) - 1, min(t0 + 1, QTB - 1))]])
        m_xmq[qpar] = m_xmq_sub[(qpar, QTB // 2 - 1)]

    for Q in range(NQ):
        qpar = Q % 2
        if Q < 2:
            m_xmq[qpar] = m_xmq_pre[Q]
            for h in range(QTB // 2):
                m_xmq_sub[(qpar, h)] = m_xmq_pre[Q]

        def xq(sp):
            if sp >= 2:
                return (Q, sp - 2)
            return (Q - 1, SQ - 2 + sp) if Q > 0 else None

        for s in range(SQ + 2):
            if s < SQ:
                ei, t = divmod(s, QTB)
                e_ = DE[ei]
                par = s % 2
                epar = ei % 2
                gb = (0, 1) if par == 0 else (2, 3)
                def gumm(e, t=t, gb=gb, epar=epar, qpar=qpar):
                    for c in range(NCH):
                        lhs = xmq[qpar][:, (c * QTB + t) * 128:(c * QTB + t) * 128 + 128]
                        e.matmul(bank(gb[0]), lhs, egu[epar][:, c * D:c * D + 512],
                                 start=(c == 0), stop=(c == NCH - 1))
                        mm = e.matmul(bank(gb[1]), lhs, egu[epar][:, c * D + 512:(c + 1) * D],
                                      start=(c == 0), stop=(c == NCH - 1))
                    return mm
                w_gu = [m_egu[(0, ei)], m_edn[(0, ei)], m_xmq_sub[(qpar, t // 2)]]
                if Q == 0 and s < 2:
                    w_gu += [fD_pe, fD_act, fD_dve_pre]
                if xq(s):
                    w_gu.append(m_act_done[xq(s)])
                m_pe_gu[(Q, ei, t)] = p.op("pe", gumm, waits=w_gu, inc=SPE)
                w_a = [m_pe_gu[(Q, ei, t)]]
                if xq(s):
                    w_a.append(m_tt[xq(s)])
                if Q == 0 and s < 2:
                    w_a += [(f"sm0", 16 * nm[0]), (f"sm1", 16 * nm[1])]
                p.op("act", lambda e, gb=gb, par=par: e.activation(
                    bfc[par][:, 0:512], bank(gb[0]), AF.Silu), waits=w_a, inc=SACT)
                m_act_done[(Q, s)] = p.op("act", lambda e, gb=gb, par=par:
                                          e.activation(bfc[par][:, 512:1024], bank(gb[1]),
                                                       AF.Identity),
                                          inc=SACT)
                w_tt = [m_act_done[(Q, s)]]
                if xq(s):
                    w_tt.append(m_pe_tr[xq(s)])
                m_tt[(Q, s)] = p.op("dve", lambda e, par=par: e.tensor_mul(
                    bfd[par][:, 0:512], bfc[par][:, 0:512], bfc[par][:, 512:1024]),
                    waits=w_tt, inc=SDVE)
            if 0 <= s - 1 < SQ:
                sp = s - 1
                par = sp % 2
                tb_ = 4 + par

                def trq(e, par=par, tb_=tb_):
                    for jj2 in range(4):
                        mm = e.transpose(bankb(tb_)[:, jj2 * 128:(jj2 + 1) * 128],
                                         bfd[par][:, jj2 * 128:(jj2 + 1) * 128], identb[:])
                    return mm
                w_tr = [m_tt[(Q, sp)]]
                if Q == 0 and sp < 2:
                    w_tr.append(fD_dve_pre)
                if xq(sp):
                    w_tr.append(m_evac[xq(sp)])
                m_pe_tr[(Q, sp)] = p.op("pe", trq, waits=w_tr, inc=SPE)
                w_ev = [m_pe_tr[(Q, sp)]]
                if xq(sp):
                    w_ev.append(m_pe_dn[xq(sp)])
                m_evac[(Q, sp)] = p.op("dve", lambda e, par=par, tb_=tb_: e.tensor_copy(
                    bfd[par][:, 512:1024], bankb(tb_)[:, 0:512]), waits=w_ev, inc=SDVE)
            if 0 <= s - 2 < SQ:
                sp = s - 2
                ei, t = divmod(sp, QTB)
                par = sp % 2
                epar = ei % 2

                def dnmm(e, par=par, epar=epar):
                    for c in range(4):
                        lhs = bfd[par][:, 512 + c * 128:512 + (c + 1) * 128]
                        e.matmul(bank(6), lhs, edn[epar][:, c * D:c * D + 512],
                                 start=(c == 0), stop=(c == 3))
                        mm = e.matmul(bank(7), lhs, edn[epar][:, c * D + 512:(c + 1) * D],
                                      start=(c == 0), stop=(c == 3))
                    return mm
                w_dn = [m_evac[(Q, sp)]]
                if Q == 0 and sp == 0:
                    w_dn.append(mbbe)
                if sp >= 1:
                    w_dn.append(m_acc[(Q, sp - 1)])
                elif Q > 0:
                    w_dn.append(m_acc[(Q - 1, SQ - 1)])
                m_pe_dn[(Q, sp)] = p.op("pe", dnmm, waits=w_dn, inc=SPE)
                if ei == 0:
                    w_ac = [m_pe_dn[(Q, sp)], m_fin_t.get(t)]
                    m_acc[(Q, sp)] = p.op("dve", lambda e, t=t: e.tensor_copy(
                        acc[:, t * D:(t + 1) * D], bank(6, 2)), waits=w_ac, inc=SDVE)
                else:
                    m_acc[(Q, sp)] = p.op("dve", lambda e, t=t: e.tensor_tensor(
                        acc[:, t * D:(t + 1) * D], acc[:, t * D:(t + 1) * D], bank(6, 2),
                        ALU.add), waits=[m_pe_dn[(Q, sp)]], inc=SDVE)
                    j = Q * QTB + t
                    par_j = j % 2
                    mro = st_o[par_j].dma(xt[par_j][:], o1_sp[j * 128:(j + 1) * 128, :],
                                          waits=[m_out_prev.get(par_j), m_o1sp[j]])
                    mfin = p.op("dve", lambda e, t=t, par_j=par_j: e.tensor_tensor(
                        yt[par_j][:], xt[par_j][:], acc[:, t * D:(t + 1) * D], ALU.add),
                        waits=[mro, m_acc[(Q, sp)],
                               (f"sy{par_j}", 16 * st_y[par_j].n)],
                        inc=SDVE)
                    m_final.append(mfin)
                    m_fin_t[t] = mfin
                    m_out_prev[par_j] = mfin
                    st_y[par_j].dma(o1_sp[j * 128:(j + 1) * 128, :], yt[par_j][:],
                                    waits=[mfin])

        if Q + 2 < NQ:
            emit_xmq_sub(Q + 2)
    f_shared = [(f"sy{i}", 16 * st_y[i].n) for i in range(2)]

    xft = [qsf, bfa]
    m_xg = {}
    m_itr = {}
    m_iev = {}
    m_rgu = {}
    m_rtr = {}
    m_rdn = {}
    m_ract = {}
    m_rtt = {}
    m_oev = {}
    m_ost = {}
    SR = 32 * TPE
    st_xg = [Stream(p, f"sx{i}g", "sy") for i in range(4)]
    st_op = [Stream(p, f"so{i}p", "gp") for i in range(4)]

    def egu_load(e_, waits):
        ep = e_ % 2
        ma = st_e[2 * ep].dma(
            egu[ep][:, 0:4 * D].rearrange("p (c f) -> p c f", c=4), _r3(egu_bf[e_][0:512]),
            waits=waits + [m_pcg[e_]])
        mb = st_e[2 * ep + 1].dma(
            egu[ep][:, 4 * D:8 * D].rearrange("p (c f) -> p c f", c=4), _r3(egu_bf[e_][512:1024]),
            waits=waits + [m_pcg[e_]])
        return [ma, mb]

    for e_ in range(2):
        m_egu[("r", e_)] = egu_load(e_, [m_pe_gu[(NQ - 1, e_, QTB - 1)]])
        m_edn[("r", e_)] = st_n[e_ % 2].dma(
            edn[e_ % 2][:].rearrange("p (c f) -> p c f", c=4), _r3(edn_bf[e_]),
            waits=[m_pe_dn[(NQ - 1, SQ - 1)], m_pcd[e_]])

    for k in range(SR + 3):
        if k >= 1 and (k - 1) % TPE == 0:
            eg = (k - 1) // TPE + 1
            if 2 <= eg < 32:
                m_egu[("r", eg)] = egu_load(eg, [m_rgu[k - 2]])
        if k < SR:
            e_, kt = divmod(k, TPE)
            par = k % 2
            par4 = k % 4
            epar = e_ % 2
            m_xg[par4] = st_xg[par4].dma(
                qfm4[par4][:], xg_dram[(e_ * CAP + kt * 128):(e_ * CAP + kt * 128 + 128), :],
                waits=f_scat4 + [m_itr.get(k - 4)])
            w_it = [m_xg[par4], m_iev.get(k - 1)]
            if k == 0:
                w_it += [m_acc[(NQ - 1, SQ - 1)], m_act_done[(NQ - 1, SQ - 1)],
                         m_evac[(NQ - 1, SQ - 1)]]
            def itr(e, par4=par4):
                for c in range(NCH):
                    mm = e.transpose(bankb(4)[:, c * 128:(c + 1) * 128],
                                     qfm4[par4][:, c * 128:(c + 1) * 128], identb[:])
                return mm
            m_itr[k] = p.op("pe", itr, waits=w_it, inc=SPE)
            m_iev[k] = p.op("dve", lambda e, par=par: e.tensor_copy(xft[par][:], bankb(4)),
                            waits=[m_itr[k], m_rgu.get(k - 2)], inc=SDVE)
        if 0 <= k - 1 < SR:
            kk = k - 1
            e_, kt = divmod(kk, TPE)
            par = kk % 2
            epar = e_ % 2
            gb = (0, 1) if par == 0 else (2, 3)

            def rgu(e, par=par, epar=epar, gb=gb):
                for c in range(NCH):
                    lhs = xft[par][:, c * 128:(c + 1) * 128]
                    e.matmul(bank(gb[0]), lhs, egu[epar][:, c * D:c * D + 512],
                             start=(c == 0), stop=(c == NCH - 1))
                    mm = e.matmul(bank(gb[1]), lhs, egu[epar][:, c * D + 512:(c + 1) * D],
                                  start=(c == 0), stop=(c == NCH - 1))
                return mm
            w_rg = [m_iev[kk], m_ract.get(kk - 2)] + m_egu[("r", e_)]
            m_rgu[kk] = p.op("pe", rgu, waits=w_rg, inc=SPE)
            p.op("act", lambda e, gb=gb, par=par: e.activation(
                bfc[par][:, 0:512], bank(gb[0]), AF.Silu), waits=[m_rgu[kk]], inc=SACT)
            m_ract[kk] = p.op("act", lambda e, gb=gb, par=par: e.activation(
                bfc[par][:, 512:1024], bank(gb[1]), AF.Identity), inc=SACT)
            m_rtt[kk] = p.op("dve", lambda e, par=par: e.tensor_mul(
                bfd[par][:, 0:512], bfc[par][:, 0:512], bfc[par][:, 512:1024]),
                waits=[m_ract[kk], m_rtr.get(kk - 2)], inc=SDVE)
        if 0 <= k - 2 < SR:
            kk = k - 2
            par = kk % 2

            def rtr(e, par=par):
                for jj2 in range(4):
                    mm = e.transpose(bankb(5)[:, jj2 * 128:(jj2 + 1) * 128],
                                     bfd[par][:, jj2 * 128:(jj2 + 1) * 128], identb[:])
                return mm
            m_rtr[kk] = p.op("pe", rtr, waits=[m_rtt[kk], m_evac.get(("r", kk - 1))], inc=SPE)
            m_evac[("r", kk)] = p.op("dve", lambda e, par=par: e.tensor_copy(
                bfd[par][:, 512:1024], bankb(5)[:, 0:512]),
                waits=[m_rtr[kk], m_rdn.get(kk - 2)], inc=SDVE)
        if 0 <= k - 3 < SR:
            kk = k - 3
            e_, kt = divmod(kk, TPE)
            par = kk % 2
            epar = e_ % 2

            def rdn(e, par=par, epar=epar):
                for c in range(4):
                    lhs = bfd[par][:, 512 + c * 128:512 + (c + 1) * 128]
                    e.matmul(bank(6), lhs, edn[epar][:, c * D:c * D + 512],
                             start=(c == 0), stop=(c == 3))
                    mm = e.matmul(bank(7), lhs, edn[epar][:, c * D + 512:(c + 1) * D],
                                  start=(c == 0), stop=(c == 3))
                return mm
            w_rd = [m_evac[("r", kk)], m_edn[("r", e_)], m_oev.get(kk - 1)]
            m_rdn[kk] = p.op("pe", rdn, waits=w_rd, inc=SPE)
            k4 = kk % 4
            w_oe = [m_rdn[kk]]
            if kk >= 4:
                w_oe.append((st_op[k4].name, 16 * st_op[k4].n))
            m_oev[kk] = p.op("dve", lambda e, k4=k4: e.tensor_copy(obf[k4][:], bank(6, 2)),
                             waits=w_oe, inc=SDVE)
            e2_, kt2 = divmod(kk, TPE)
            m_ost[kk] = st_op[k4].dma(
                outp_dram[(e2_ * CAP + kt2 * 128):(e2_ * CAP + kt2 * 128 + 128), :],
                obf[k4][:], waits=[m_oev[kk]])
            if kt2 == TPE - 1 and e2_ + 2 < 32:
                en = e2_ + 2
                m_edn[("r", en)] = st_n[en % 2].dma(
                    edn[en % 2][:].rearrange("p (c f) -> p c f", c=4), _r3(edn_bf[en]),
                    waits=[m_rdn[kk], m_pcd[en]])
    f_routed = [(s.name, 16 * s.n) for s in st_op]

    gq = st_gt + st_s4 + st_pc + st_e + st_n
    gbuf = ybb + obf + gbt[:14]
    NG = 22
    m_g_prev = [None] * NG
    g_used = [False] * NG
    ring = [xt[0], xt[1], yt[0], yt[1]]
    rq = st_og + st_yr
    m_fin_prev = [None] * 4
    m_ro = {}
    for j in range(NTB_OWN):
        r4 = j % 4
        w_ro = [m_fin_prev[r4]] + f_shared
        if j >= 4:
            w_ro.append((f"sw{j % 2}", 16 * (j // 2 - 1)))
        mro = rq[r4].dma(ring[r4][:], o1_sp[j * 128:(j + 1) * 128, :], waits=w_ro)
        macc = mro
        for jj in range(6):
            gpar = (j * 6 + jj) % NG
            sq = gq[gpar]
            w_g = [m_pos, m_g_prev[gpar], (sq.name, 16 * sq.n) if sq.n else None]
            if not g_used[gpar]:
                w_g += f_routed
                g_used[gpar] = True
            mg_ = p.op("gp", lambda e, j=j, jj=jj, gpar=gpar: e.indirect_dma_start(
                out=gbuf[gpar][:], out_offset=None,
                in_=outp_dram,
                in_offset=IndirectOffsetOnAxis(ap=posi[:, jj * NT + j:jj * NT + j + 1], axis=0)),
                waits=w_g, inc=sq.name, inc_n=16)
            sq.n += 1
            macc = p.op("dve", lambda e, j=j, jj=jj, gpar=gpar, r4=r4:
                        e.scalar_tensor_tensor(ring[r4][:], gbuf[gpar][:],
                                               wgtj[:, jj * NT + j:jj * NT + j + 1],
                                               ring[r4][:], ALU.mult, ALU.add),
                        waits=[mg_, macc], inc=SDVE)
            m_g_prev[gpar] = macc
        m_fin_prev[r4] = macc
        st_w[j % 2].dma(out_dram[j * 128:(j + 1) * 128, :], ring[r4][:], waits=[macc])

    p.op("sy", None, waits=[(f"sw{i}", 16 * st_w[i].n) for i in range(2)])

    import contextlib
    stack = contextlib.ExitStack()
    sems = {name: stack.enter_context(nc.semaphore(name)) for name in sorted(p.sem_names)}
    with nc.Block() as block:
        p.emit(block, sems)
    return nc


_BF = ml_dtypes.bfloat16


def _stage(inputs):
    x = np.asarray(inputs["L0_attn_x"])
    f32 = np.float32

    def c_(a):
        return np.ascontiguousarray(np.asarray(a, f32))

    def cols(v, n):
        return np.ascontiguousarray(np.asarray(v, f32).reshape(n, 128).T)

    def rep(v):
        v = np.asarray(v, f32)
        return np.ascontiguousarray(np.broadcast_to(v[None, :], (128, v.shape[0])))

    gwT = c_(np.asarray(inputs["L0_ffn_gate_weight"], f32).T)
    sgT = np.asarray(inputs["L0_ffn_shared_gate_w"], f32).T
    suT = np.asarray(inputs["L0_ffn_shared_up_w"], f32).T
    sdT = np.asarray(inputs["L0_ffn_shared_down_w"], f32).T
    eguT = np.asarray(inputs["L0_ffn_experts_gate_up"], f32).transpose(0, 2, 1)
    ednT = np.asarray(inputs["L0_ffn_experts_down"], f32).transpose(0, 2, 1)
    sh32 = np.concatenate([sgT[:, 0:512], suT[:, 0:512]], axis=1)[None]
    sh33 = np.concatenate([sgT[:, 512:1024], suT[:, 512:1024]], axis=1)[None]
    egu = np.ascontiguousarray(np.concatenate([eguT, sh32, sh33], axis=0))
    edn = np.ascontiguousarray(np.concatenate([ednT, sdT[None, 0:512], sdT[None, 512:1024]],
                                              axis=0))
    common = dict(
        wq=c_(inputs["L0_attn_wq"]), wk=c_(inputs["L0_attn_wk"]), wv=c_(inputs["L0_attn_wv"]),
        wo=c_(inputs["L0_attn_wo"]), wd=c_(inputs["L0_attn_w_d"]),
        ff1=c_(inputs["L0_attn_w_ff1"]), ff2=c_(inputs["L0_attn_w_ff2"]),
        g1=cols(inputs["L0_attn_ln1_g"], 8), b1=cols(inputs["L0_attn_ln1_b"], 8),
        g2=cols(inputs["L0_attn_ln2_g"], 8), b2=cols(inputs["L0_attn_ln2_b"], 8),
        bff1=cols(inputs["L0_attn_b_ff1"], 32),
        bo_rep=rep(inputs["L0_attn_bo"]),
        bff2_rep=rep(inputs["L0_attn_b_ff2"]),
        bd_rep=rep(inputs["L0_attn_b_d"]),
        esc_rep=rep(inputs["L0_ffn_e_score_correction_bias"]),
        gwT=gwT, egu=egu, edn=edn,
        identb=np.eye(128, dtype=_BF), identf=np.eye(128, dtype=np.float32),
        lt128=np.triu(np.ones((128, 128), _BF)),
        lt32=np.triu(np.ones((32, 32), _BF), 1),
        eoffm1=rep(np.arange(32, dtype=np.float32) * CAP - 1),
    )
    in_maps = []
    for core in range(8):
        b, qtr = divmod(core, 4)
        xb = np.asarray(x[b], f32)
        xc = np.ascontiguousarray(np.concatenate(
            [xb[qtr * 4096:(qtr + 1) * 4096], xb[:qtr * 4096], xb[(qtr + 1) * 4096:]], axis=0))
        in_maps.append(dict(common, x=xc))
    return in_maps


def kernel(**inputs):
    in_maps = _stage(inputs)
    nc = build()
    res = run_bass_kernel_spmd(nc, in_maps, core_ids=list(range(8)))
    out = np.empty((2, 16384, 1024), np.float32)
    for core in range(8):
        b, qtr = divmod(core, 4)
        out[b, qtr * 4096:(qtr + 1) * 4096] = res.results[core]["out"]
    return out


if __name__ == "__main__":
    nc = build()
    print("instructions:", sum(1 for _ in nc.all_instructions()))
